# version 08
# speedup vs baseline: 1.2233x; 1.0301x over previous
ENGINES = ("sync", "tensor", "vector", "scalar", "gpsimd")
NSLOT = 4
SEM_EPOCH = 30000
NEPOCH = 4


class Plan:
    def __init__(self, nc, ctx):
        self.nc = nc
        self.ctx = ctx
        self.ops = {e: [] for e in ENGINES}
        self.slot_count = {e: [0] * NSLOT for e in ENGINES}
        self.ndma = {e: 0 for e in ENGINES}
        self.last_w = {}
        self.readers = {}

    @staticmethod
    def key(ap_or_handle):
        t = getattr(ap_or_handle, "tensor", ap_or_handle)
        n = getattr(t, "name", None)
        return n if n is not None else str(t)

    def add(self, eng, fn, reads=(), writes=(), dma=False, tag="",
            extra_waits=()):
        fv = getattr(getattr(fn, "__code__", None), "co_freevars", ())
        assert not fv, (
            f"op lambda has closure free variables {fv} (tag={tag}): bind "
            f"them as default args — lambdas run at emit time, after loop "
            f"variables have been reassigned")
        idx = len(self.ops[eng])
        kind = "d" if dma else "c"
        waits = {}

        def need(k2, e2, i2):
            if e2 == eng and k2 == "c" and eng == "tensor":
                return
            if k2 == "d":
                for slot, o in enumerate(self.ops[e2][i2]["cum"]):
                    if o > 0:
                        kk = ("d", e2, slot)
                        if o > waits.get(kk, 0):
                            waits[kk] = o
            else:
                kk = ("c", e2)
                if i2 > waits.get(kk, -1):
                    waits[kk] = i2

        for r in reads:
            lw = self.last_w.get(self.key(r))
            if lw:
                need(*lw)
        for w in writes:
            k = self.key(w)
            lw = self.last_w.get(k)
            if lw:
                need(*lw)
            for (k2, e2), i2 in self.readers.get(k, {}).items():
                need(k2, e2, i2)
        for t in extra_waits:
            need(*t)
        op = dict(fn=fn, waits=waits, kind=kind, marked=False, tag=tag)
        if dma:
            slot = self.ndma[eng] % NSLOT
            self.slot_count[eng][slot] += 1
            op["slot"] = slot
            op["ord"] = self.slot_count[eng][slot]
            op["cum"] = list(self.slot_count[eng])
            self.ndma[eng] += 1
        self.ops[eng].append(op)
        me = (kind, eng)
        for r in reads:
            d = self.readers.setdefault(self.key(r), {})
            if d.get(me, -1) < idx:
                d[me] = idx
        for w in writes:
            k = self.key(w)
            self.last_w[k] = (kind, eng, idx)
            self.readers[k] = {}
        return (kind, eng, idx)

    def snapshot(self):
        targets = []
        for e in ENGINES:
            for i in range(len(self.ops[e]) - 1, -1, -1):
                if self.ops[e][i]["kind"] == "c":
                    targets.append(("c", e, i))
                    break
            for i in range(len(self.ops[e]) - 1, -1, -1):
                if self.ops[e][i]["kind"] == "d":
                    targets.append(("d", e, i))
                    break
        return targets

    def full_barrier(self, tag="barrier"):
        targets = self.snapshot()
        for e in ENGINES:
            self.add(e, lambda h: h.nop(), extra_waits=targets, tag=tag)

    def emit(self, verbose=False):
        nc, ctx = self.nc, self.ctx
        for e in ENGINES:
            for op in self.ops[e]:
                for kk, v in op["waits"].items():
                    if kk[0] == "c":
                        self.ops[kk[1]][v]["marked"] = True
        cvals = {}
        for e in ENGINES:
            cv, cl = 0, []
            for op in self.ops[e]:
                if op["kind"] == "d":
                    op["marked"] = True
                elif op["marked"]:
                    cv += 1
                cl.append(cv)
            cvals[e] = cl
        csem = {e: [ctx.enter_context(nc.semaphore(f"c_{e}{k}")) for k in range(NEPOCH)]
                for e in ENGINES
                if any(o["kind"] == "c" and o["marked"] for o in self.ops[e])}
        dsem = {e: [ctx.enter_context(nc.semaphore(f"d_{e}{s}"))
                    for s in range(NSLOT)]
                for e in ENGINES if self.ndma[e]}
        nc.all_engine_barrier()
        for s_ in ([x for l in csem.values() for x in l]
                   + [x for l in dsem.values() for x in l]):
            nc.gpsimd.sem_clear(s_)
        nc.all_engine_barrier()
        block = ctx.enter_context(nc.Block())
        stats = dict(n_ops=0, n_waits=0)

        cnt = {e: 0 for e in ENGINES}

        def body(e):
            def run(h):
                seen = {}
                for op in self.ops[e]:
                    if op["kind"] == "d" and op["ord"] > 1:
                        kk = ("d", e, op["slot"])
                        v = 16 * (op["ord"] - 1)
                        if seen.get(kk, 0) < v:
                            h.wait_ge(dsem[e][op["slot"]], v)
                            seen[kk] = v
                            stats["n_waits"] += 1
                    for kk in sorted(op["waits"], key=str):
                        if kk[0] == "c":
                            cv = cvals[kk[1]][op["waits"][kk]]
                            ep = (cv - 1) // SEM_EPOCH
                            v, sem = cv - ep * SEM_EPOCH, csem[kk[1]][ep]
                            kk2 = ("c", kk[1], ep)
                        else:
                            v, sem = 16 * op["waits"][kk], dsem[kk[1]][kk[2]]
                            kk2 = kk
                        if v == 0 or seen.get(kk2, 0) >= v:
                            continue
                        h.wait_ge(sem, v)
                        stats["n_waits"] += 1
                        seen[kk2] = v
                    instr = op["fn"](h)
                    stats["n_ops"] += 1
                    if op["marked"]:
                        assert instr is not None, \
                            f"marked op emitted no instruction: {e} #{op['tag']}"
                        if op["kind"] == "d":
                            instr.then_inc(dsem[e][op["slot"]], 16)
                        else:
                            cnt[e] += 1
                            ep = (cnt[e] - 1) // SEM_EPOCH
                            instr.then_inc(csem[e][ep], 1)
            return run

        for e in ENGINES:
            if self.ops[e]:
                getattr(block, e)(body(e))
        return stats


"""Stage emitters for the mk_recurrent_d2 megakernel. Will be inlined into
kernel.py (kernel.py must be self-contained).

Conventions:
- Residual stream is FEATURE-MAJOR per sequence: X^T stored in DRAM as
  [NSEQ, D=1024, T=2048] fp32 (d split into 8 chunks of 128 on partitions).
- NSEQ = 4 sequences per core.
- All weights arrive pre-transposed by the host so the device DMAs
  [in_dim(part) x out_dim(free)] tiles contiguously.
"""
import numpy as np

import concourse.bass as bass
import concourse.mybir as mybir

F32 = mybir.dt.float32
BF16 = mybir.dt.bfloat16
AF = mybir.ActivationFunctionType
ALU = mybir.AluOpType
AX = mybir.AxisListType

NSEQ = 4
T = 2048
D = 1024
NCH = D // 128


def sb(nc, ctx, name, shape, dtype=F32):
    return ctx.enter_context(nc.sbuf_tensor(name, shape, dtype))


def ps(nc, ctx, name, shape, dtype=F32):
    return ctx.enter_context(nc.psum_tensor(name, shape, dtype))


class ConstCols:

    def __init__(self, nc, ctx, cap=32):
        self.vals = []
        self.tile = sb(nc, ctx, "c_cols_sb", [128, cap])
        self.cap = cap

    def col(self, v):
        v = float(v)
        if v not in self.vals:
            assert len(self.vals) < self.cap
            self.vals.append(v)
        i = self.vals.index(v)
        return self.tile[:, i:i + 1]

    def load(self, p, dram_ap):
        self.dram = dram_ap
        p.add("sync", lambda h, dram_ap=dram_ap, self=self: h.dma_start(out=self.tile[:], in_=dram_ap),
              reads=[dram_ap], writes=[self.tile], dma=True, tag="ldconsts")

    def host_array(self):
        a = np.zeros((128, self.cap), np.float32)
        a[:, :len(self.vals)] = np.array(self.vals, np.float32)[None, :]
        return a


def emit_outer_ln(p, outer_ctx, x_dram, xn_dram, ones128, cc, tag):
    nc = p.nc
    from contextlib import ExitStack
    ctx = ExitStack()
    _scope = ctx
    TT = 512
    NTT = T // TT
    xt = [sb(nc, ctx, f"{tag}_x{b}", [128, NCH * TT]) for b in range(2)]
    xnt = [sb(nc, ctx, f"{tag}_xn{b}", [128, NCH * TT]) for b in range(2)]
    xsq = sb(nc, ctx, f"{tag}_xsq", [128, NCH * TT])
    mu = [sb(nc, ctx, f"{tag}_mu{b}", [128, TT]) for b in range(2)]
    musq = sb(nc, ctx, f"{tag}_musq", [128, TT])
    var = sb(nc, ctx, f"{tag}_var", [128, TT])
    lnv = sb(nc, ctx, f"{tag}_lnv", [128, TT])
    rstd = [sb(nc, ctx, f"{tag}_rstd{b}", [128, TT]) for b in range(2)]
    psum_s = [ps(nc, ctx, f"{tag}_ps{b}", [128, TT]) for b in range(2)]
    psum_q = [ps(nc, ctx, f"{tag}_pq{b}", [128, TT]) for b in range(2)]

    for i, (s, j) in enumerate([(s, j) for s in range(NSEQ) for j in range(NTT)]):
        b = i % 2
        src = x_dram[s].rearrange("(c p) t -> p c t", p=128)[:, :, j * TT:(j + 1) * TT]
        dst = xn_dram[s].rearrange("(c p) t -> p c t", p=128)[:, :, j * TT:(j + 1) * TT]
        xv = xt[b][:].rearrange("p (c t) -> p c t", t=TT)
        xnv = xnt[b][:].rearrange("p (c t) -> p c t", t=TT)
        p.add("sync", lambda h, src=src, b=b, xt=xt: ld3(h, xt[b], src),
              reads=[x_dram], writes=[xt[b]], dma=True, tag=f"{tag}ld{i}")
        p.add("scalar", lambda h, b=b, xsq=xsq, xt=xt: h.activation(out=xsq[:], in_=xt[b][:], func=AF.Square),
              reads=[xt[b]], writes=[xsq], tag=f"{tag}sq{i}")
        for c in range(NCH):
            p.add("tensor", lambda h, b=b, c=c, xv=xv, ones128=ones128, psum_s=psum_s: h.matmul(
                out=psum_s[b][:], lhsT=ones128[:], rhs=xv[:, c, :],
                start=(c == 0), stop=(c == NCH - 1)),
                reads=[xt[b], ones128], writes=[psum_s[b]], tag=f"{tag}mmS{i}.{c}")
        for c in range(NCH):
            p.add("tensor", lambda h, b=b, c=c, TT=TT, ones128=ones128, psum_q=psum_q, xsq=xsq: h.matmul(
                out=psum_q[b][:], lhsT=ones128[:],
                rhs=xsq[:].rearrange("p (c t) -> p c t", t=TT)[:, c, :],
                start=(c == 0), stop=(c == NCH - 1)),
                reads=[xsq, ones128], writes=[psum_q[b]], tag=f"{tag}mmQ{i}.{c}")
        p.add("vector", lambda h, b=b, mu=mu, psum_s=psum_s: h.tensor_scalar_mul(out=mu[b][:], in0=psum_s[b][:], scalar1=1.0 / D),
              reads=[psum_s[b]], writes=[mu[b]], tag=f"{tag}mu{i}")
        p.add("vector", lambda h, b=b, mu=mu, musq=musq: h.tensor_mul(out=musq[:], in0=mu[b][:], in1=mu[b][:]),
              reads=[mu[b]], writes=[musq], tag=f"{tag}musq{i}")
        p.add("vector", lambda h, b=b, musq=musq, psum_q=psum_q, var=var: h.scalar_tensor_tensor(
            out=var[:], in0=psum_q[b][:], scalar=1.0 / D, in1=musq[:],
            op0=ALU.mult, op1=ALU.subtract),
            reads=[psum_q[b], musq], writes=[var], tag=f"{tag}var{i}")
        p.add("scalar", lambda h, cc=cc, lnv=lnv, var=var: h.activation(out=lnv[:], in_=var[:], func=AF.Ln, bias=cc.col(1e-5)),
              reads=[var, cc.tile], writes=[lnv], tag=f"{tag}ln{i}")
        p.add("scalar", lambda h, b=b, cc=cc, lnv=lnv, rstd=rstd: h.activation(out=rstd[b][:], in_=lnv[:], func=AF.Exp, scale=cc.col(-0.5)),
              reads=[lnv, cc.tile], writes=[rstd[b]], tag=f"{tag}rstd{i}")
        p.add("vector", lambda h, b=b, xv=xv, xnv=xnv, TT=TT, mu=mu: h.tensor_tensor(
            out=xnv, in0=xv,
            in1=mu[b][:].unsqueeze(1).broadcast_to([128, NCH, TT]), op=ALU.subtract),
            reads=[xt[b], mu[b]], writes=[xnt[b]], tag=f"{tag}sub{i}")
        p.add("vector", lambda h, b=b, xnv=xnv, TT=TT, rstd=rstd: h.tensor_tensor(
            out=xnv, in0=xnv,
            in1=rstd[b][:].unsqueeze(1).broadcast_to([128, NCH, TT]), op=ALU.mult),
            reads=[xnt[b], rstd[b]], writes=[xnt[b]], tag=f"{tag}mul{i}")
        p.add("gpsimd", lambda h, b=b, dst=dst, xnt=xnt: st3(h, dst, xnt[b]),
              reads=[xnt[b]], writes=[xn_dram], dma=True, tag=f"{tag}st{i}")
    p.full_barrier(f"{tag}_end")
    _scope.close()


def load_weight_bf16(p, ctx, staging, arena, arena_off, dram2d, in_dim, out_dim, tag):
    nc = p.nc
    C = in_dim // 128
    SL = staging[0].shape[1]
    view = dram2d.rearrange("(c p) f -> p c f", p=128)
    i = 0
    for c in range(C):
        for f0 in range(0, out_dim, SL):
            fl = min(SL, out_dim - f0)
            st = staging[i % 2]
            p.add("sync", lambda h, st=st, c=c, f0=f0, fl=fl, view=view: h.dma_start(
                out=st[:, 0:fl], in_=view[:, c, f0:f0 + fl]),
                reads=[dram2d], writes=[st], dma=True, tag=f"{tag}wld{i}")
            off = arena_off + c * out_dim + f0
            p.add("vector", lambda h, st=st, off=off, fl=fl, arena=arena: h.tensor_copy(
                out=arena[:, off:off + fl], in_=st[:, 0:fl]),
                reads=[st], writes=[arena], tag=f"{tag}wcast{i}")
            i += 1


class LNScratch:

    def __init__(self, nc, ctx, tag, TT, psum_pair):
        self.TT = TT
        self.xsq = sb(nc, ctx, f"{tag}_xsq", [128, NCH * TT])
        self.mu = sb(nc, ctx, f"{tag}_mu", [128, TT])
        self.musq = sb(nc, ctx, f"{tag}_musq", [128, TT])
        self.rstd = sb(nc, ctx, f"{tag}_rstd", [128, TT])
        self.psum = psum_pair


def emit_ln_fm_tile(p, ln, src, dst, ones128, cc, tag, gamma=None, beta=None,
                    eps=1e-5, nch=NCH):
    TT = ln.TT
    sv = src[:].rearrange("p (c t) -> p c t", t=TT)
    dv = dst[:].rearrange("p (c t) -> p c t", t=TT)
    tv = ln.xsq[:].rearrange("p (c t) -> p c t", t=TT)
    p.add("scalar", lambda h, TT=TT, ln=ln, nch=nch, src=src: h.activation(out=ln.xsq[:, 0:nch * TT], in_=src[:, 0:nch * TT], func=AF.Square),
          reads=[src], writes=[ln.xsq], tag=f"{tag}sq")
    for c in range(nch):
        p.add("tensor", lambda h, c=c, ln=ln, nch=nch, ones128=ones128, sv=sv: h.matmul(
            out=ln.psum[0][:], lhsT=ones128[:], rhs=sv[:, c, :],
            start=(c == 0), stop=(c == nch - 1)),
            reads=[src, ones128], writes=[ln.psum[0]], tag=f"{tag}mmS{c}")
    for c in range(nch):
        p.add("tensor", lambda h, c=c, ln=ln, nch=nch, ones128=ones128, tv=tv: h.matmul(
            out=ln.psum[1][:], lhsT=ones128[:], rhs=tv[:, c, :],
            start=(c == 0), stop=(c == nch - 1)),
            reads=[ln.xsq, ones128], writes=[ln.psum[1]], tag=f"{tag}mmQ{c}")
    p.add("vector", lambda h, ln=ln, nch=nch: h.tensor_scalar_mul(out=ln.mu[:], in0=ln.psum[0][:], scalar1=1.0 / (nch * 128)),
          reads=[ln.psum[0]], writes=[ln.mu], tag=f"{tag}mu")
    p.add("vector", lambda h, ln=ln: h.tensor_mul(out=ln.musq[:], in0=ln.mu[:], in1=ln.mu[:]),
          reads=[ln.mu], writes=[ln.musq], tag=f"{tag}musq")
    p.add("vector", lambda h, ln=ln, nch=nch: h.scalar_tensor_tensor(
        out=ln.musq[:], in0=ln.psum[1][:], scalar=1.0 / (nch * 128), in1=ln.musq[:],
        op0=ALU.mult, op1=ALU.subtract),
        reads=[ln.psum[1], ln.musq], writes=[ln.musq], tag=f"{tag}var")
    p.add("scalar", lambda h, cc=cc, eps=eps, ln=ln: h.activation(out=ln.musq[:], in_=ln.musq[:], func=AF.Ln, bias=cc.col(eps)),
          reads=[ln.musq, cc.tile], writes=[ln.musq], tag=f"{tag}lnv")
    p.add("scalar", lambda h, cc=cc, ln=ln: h.activation(out=ln.rstd[:], in_=ln.musq[:], func=AF.Exp, scale=cc.col(-0.5)),
          reads=[ln.musq, cc.tile], writes=[ln.rstd], tag=f"{tag}rstd")
    p.add("vector", lambda h, TT=TT, ln=ln, nch=nch, sv=sv, tv=tv: h.tensor_tensor(
        out=tv, in0=sv, in1=ln.mu[:].unsqueeze(1).broadcast_to([128, nch, TT]),
        op=ALU.subtract), reads=[src, ln.mu], writes=[ln.xsq], tag=f"{tag}sub")
    if gamma is None:
        p.add("vector", lambda h, TT=TT, dv=dv, ln=ln, nch=nch, tv=tv: h.tensor_tensor(
            out=dv, in0=tv, in1=ln.rstd[:].unsqueeze(1).broadcast_to([128, nch, TT]),
            op=ALU.mult), reads=[ln.xsq, ln.rstd], writes=[dst], tag=f"{tag}mul")
    else:
        p.add("vector", lambda h, TT=TT, ln=ln, nch=nch, tv=tv: h.tensor_tensor(
            out=tv, in0=tv, in1=ln.rstd[:].unsqueeze(1).broadcast_to([128, nch, TT]),
            op=ALU.mult), reads=[ln.xsq, ln.rstd], writes=[ln.xsq], tag=f"{tag}mul")
        gt, g0 = gamma
        bt, b0 = beta
        for c in range(nch):
            p.add("vector", lambda h, c=c, b0=b0, bt=bt, dv=dv, g0=g0, gt=gt, tv=tv: h.tensor_scalar(
                out=dv[:, c, :], in0=tv[:, c, :], scalar1=gt[:, g0 + c:g0 + c + 1],
                scalar2=bt[:, b0 + c:b0 + c + 1], op0=ALU.mult, op1=ALU.add),
                reads=[ln.xsq, gt, bt], writes=[dst], tag=f"{tag}gb{c}")




def ld3(h, dst_t, src3):
    c, t = src3.shape[1], src3.shape[2]
    return h.dma_start(out=dst_t[:, 0:c * t].rearrange("p (c t) -> p c t", t=t), in_=src3)


def st3(h, dst3, src_t):
    c, t = dst3.shape[1], dst3.shape[2]
    return h.dma_start(out=dst3, in_=src_t[:, 0:c * t].rearrange("p (c t) -> p c t", t=t))


def fm_slice(dram3d, s, j, TT, nch=None):
    return dram3d[s].rearrange("(c p) t -> p c t", p=128)[:, :, j * TT:(j + 1) * TT]


def emit_l0_attn(p, ctx, io, cc, tag="l0a"):
    nc = p.nc
    from contextlib import ExitStack
    ctx = ExitStack()
    TT = 512
    NTT = T // TT
    NH = 8

    arena = sb(nc, ctx, f"{tag}_arena", [128, 32768], BF16)
    staging = [sb(nc, ctx, f"{tag}_stg{b}", [128, 1024]) for b in range(2)]
    ones_sb = sb(nc, ctx, f"{tag}_ones", [128, 128])
    onesbf_sb = sb(nc, ctx, f"{tag}_onesbf", [128, 128], BF16)
    vecs = sb(nc, ctx, f"{tag}_vecs", [128, 72])
    p.add("sync", lambda h, io=io, ones_sb=ones_sb: h.dma_start(out=ones_sb[:], in_=io["ones"]),
          reads=[io["ones"]], writes=[ones_sb], dma=True, tag=f"{tag}ldones")
    p.add("sync", lambda h, io=io, onesbf_sb=onesbf_sb: h.dma_start(out=onesbf_sb[:], in_=io["onesbf"]),
          reads=[io["onesbf"]], writes=[onesbf_sb], dma=True, tag=f"{tag}ldonesbf")
    p.add("sync", lambda h, io=io, vecs=vecs: h.dma_start(out=vecs[:], in_=io["vecs"]),
          reads=[io["vecs"]], writes=[vecs], dma=True, tag=f"{tag}ldvecs")

    kv_sb = sb(nc, ctx, f"{tag}_kvsb", [128, NSEQ * NH * 128], BF16)
    ks_sb = sb(nc, ctx, f"{tag}_kssb", [128, NSEQ * NH * 128], BF16)
    qdr = io["qT"]

    with ExitStack() as sctx:
        for m in range(3):
            load_weight_bf16(p, sctx, staging, arena, m * 8192, io["wqkvT"][m],
                             1024, 1024, f"{tag}w{m}")
        wq_v = arena[:, 0:24576].rearrange("p (w c f) -> p w c f", w=3, f=1024)
        xt = [sb(nc, sctx, f"{tag}A_x{b}", [128, NCH * TT]) for b in range(2)]
        x2 = [sb(nc, sctx, f"{tag}A_x2{b}", [128, NCH * TT], BF16) for b in range(2)]
        qbf = [sb(nc, sctx, f"{tag}A_q{b}", [128, NCH * TT], BF16) for b in range(2)]
        kbf = sb(nc, sctx, f"{tag}A_k", [128, 1024], BF16)
        vbf = sb(nc, sctx, f"{tag}A_v", [128, 1024], BF16)
        el1 = [sb(nc, sctx, f"{tag}A_el1{b}", [128, TT]) for b in range(2)]
        el2 = [sb(nc, sctx, f"{tag}A_el2{b}", [128, TT]) for b in range(2)]
        ps_ln = [ps(nc, sctx, f"{tag}A_ps{b}", [128, TT]) for b in range(2)]
        ps_prod = [ps(nc, sctx, f"{tag}A_pp{b}", [128, TT]) for b in range(2)]
        ps_kv = ps(nc, sctx, f"{tag}A_pkv", [128, 1024])
        ps_ks = ps(nc, sctx, f"{tag}A_pks", [128, 1024])
        ln = LNScratch(nc, sctx, f"{tag}A_ln", TT, ps_ln)

        for s in range(NSEQ):
            for j in range(NTT):
                i = s * NTT + j
                b = i % 2
                p.add("sync", lambda h, src=fm_slice(io["xn"], s, j, TT), b=b, xt=xt:
                      ld3(h, xt[b], src),
                      reads=[io["xn"]], writes=[xt[b]], dma=True, tag=f"{tag}Ald{i}")
                emit_ln_fm_tile(p, ln, xt[b], x2[b], ones_sb, cc, f"{tag}Aln{i}",
                                gamma=(vecs, 0), beta=(vecs, 8))
                x2v = x2[b][:].rearrange("p (c t) -> p c t", t=TT)
                qv = qbf[b][:].rearrange("p (c t) -> p c t", t=TT)
                for fc in range(NCH):
                    pb = ps_ln[fc % 2]
                    for dc in range(NCH):
                        p.add("tensor", lambda h, fc=fc, dc=dc, pb=pb, x2v=x2v, wq_v=wq_v: h.matmul(
                            out=pb[:], lhsT=wq_v[:, 0, dc, fc * 128:(fc + 1) * 128],
                            rhs=x2v[:, dc, :], start=(dc == 0), stop=(dc == NCH - 1)),
                            reads=[x2[b], arena], writes=[pb], tag=f"{tag}Aq{i}.{fc}.{dc}")
                    e1, e2 = el1[fc % 2], el2[fc % 2]
                    p.add("vector", lambda h, pb=pb, e1=e1: h.tensor_scalar_min(out=e1[:], in0=pb[:], scalar1=0.0),
                          reads=[pb], writes=[e1], tag=f"{tag}Aqe1{i}.{fc}")
                    p.add("scalar", lambda h, e1=e1, e2=e2: h.activation(out=e2[:], in_=e1[:], func=AF.Exp),
                          reads=[e1], writes=[e2], tag=f"{tag}Aqe2{i}.{fc}")
                    p.add("vector", lambda h, pb=pb, fc=fc, qv=qv, e2=e2: h.scalar_tensor_tensor(
                        out=qv[:, fc, :], in0=pb[:], scalar=0.0, in1=e2[:],
                        op0=ALU.max, op1=ALU.add),
                        reads=[pb, e2], writes=[qbf[b]], tag=f"{tag}Aqe3{i}.{fc}")
                p.add("gpsimd", lambda h, qdst=fm_slice(qdr, s, j, TT), b=b, qbf=qbf:
                      st3(h, qdst, qbf[b]),
                      reads=[qbf[b]], writes=[qdr], dma=True, tag=f"{tag}Aqst{i}")
                for ts4 in range(4):
                    t0 = ts4 * 128
                    gf = (j == 0 and ts4 == 0)
                    gl = (j == NTT - 1 and ts4 == 3)
                    for half in range(2):
                        pb = ps_prod[half]
                        for dc in range(NCH):
                            p.add("tensor", lambda h, dc=dc, half=half, pb=pb, x2v=x2v, t0=t0, wq_v=wq_v: h.matmul(
                                out=pb[:], lhsT=x2v[:, dc, t0:t0 + 128],
                                rhs=wq_v[:, 1, dc, half * 512:(half + 1) * 512],
                                start=(dc == 0), stop=(dc == NCH - 1)),
                                reads=[x2[b], arena], writes=[pb], tag=f"{tag}Ak{i}.{ts4}.{half}.{dc}")
                        e1, e2 = el1[half], el2[half]
                        p.add("vector", lambda h, pb=pb, e1=e1: h.tensor_scalar_min(out=e1[:], in0=pb[:], scalar1=0.0),
                              reads=[pb], writes=[e1], tag=f"{tag}Ake1{i}.{ts4}.{half}")
                        p.add("scalar", lambda h, e1=e1, e2=e2: h.activation(out=e2[:], in_=e1[:], func=AF.Exp),
                              reads=[e1], writes=[e2], tag=f"{tag}Ake2{i}.{ts4}.{half}")
                        p.add("vector", lambda h, pb=pb, half=half, e2=e2, kbf=kbf: h.scalar_tensor_tensor(
                            out=kbf[:, half * 512:(half + 1) * 512], in0=pb[:], scalar=0.0,
                            in1=e2[:], op0=ALU.max, op1=ALU.add),
                            reads=[pb, e2], writes=[kbf], tag=f"{tag}Ake3{i}.{ts4}.{half}")
                        for dc in range(NCH):
                            p.add("tensor", lambda h, dc=dc, half=half, pb=pb, x2v=x2v, t0=t0, wq_v=wq_v: h.matmul(
                                out=pb[:], lhsT=x2v[:, dc, t0:t0 + 128],
                                rhs=wq_v[:, 2, dc, half * 512:(half + 1) * 512],
                                start=(dc == 0), stop=(dc == NCH - 1)),
                                reads=[x2[b], arena], writes=[pb], tag=f"{tag}Av{i}.{ts4}.{half}.{dc}")
                        p.add("scalar", lambda h, pb=pb, half=half, vbf=vbf: h.activation(
                            out=vbf[:, half * 512:(half + 1) * 512], in_=pb[:], func=AF.Copy),
                            reads=[pb], writes=[vbf], tag=f"{tag}Ave{i}.{ts4}.{half}")
                    for hh in range(NH):
                        p.add("tensor", lambda h, hh=hh, gf=gf, gl=gl, kbf=kbf, ps_kv=ps_kv, vbf=vbf: h.matmul(
                            out=ps_kv[:, hh * 128:(hh + 1) * 128],
                            lhsT=kbf[:, hh * 128:(hh + 1) * 128],
                            rhs=vbf[:, hh * 128:(hh + 1) * 128],
                            start=(gf and hh % 4 == 0), stop=(gl and hh % 4 == 3)),
                            reads=[kbf, vbf], writes=[ps_kv], tag=f"{tag}Akv{i}.{ts4}.{hh}")
                    for hh in range(NH):
                        p.add("tensor", lambda h, hh=hh, gf=gf, gl=gl, kbf=kbf, onesbf_sb=onesbf_sb, ps_ks=ps_ks: h.matmul(
                            out=ps_ks[:, hh * 128:(hh + 1) * 128],
                            lhsT=kbf[:, hh * 128:(hh + 1) * 128],
                            rhs=onesbf_sb[:],
                            start=(gf and hh % 4 == 0), stop=(gl and hh % 4 == 3)),
                            reads=[kbf, onesbf_sb], writes=[ps_ks], tag=f"{tag}Aks{i}.{ts4}.{hh}")
            p.add("vector", lambda h, s=s, kv_sb=kv_sb, ps_kv=ps_kv: h.tensor_copy(
                out=kv_sb[:, s * 1024:(s + 1) * 1024], in_=ps_kv[:]),
                reads=[ps_kv], writes=[kv_sb], tag=f"{tag}Akvev{s}")
            p.add("scalar", lambda h, s=s, ks_sb=ks_sb, ps_ks=ps_ks: h.activation(
                out=ks_sb[:, s * 1024:(s + 1) * 1024], in_=ps_ks[:], func=AF.Copy),
                reads=[ps_ks], writes=[ks_sb], tag=f"{tag}Aksev{s}")
        p.full_barrier(f"{tag}_endA")

    with ExitStack() as sctx:
        xt = [sb(nc, sctx, f"{tag}B_x{b}", [128, NCH * TT]) for b in range(2)]
        qt = [sb(nc, sctx, f"{tag}B_q{b}", [128, NCH * TT], BF16) for b in range(2)]
        smid = [sb(nc, sctx, f"{tag}B_sm{b}", [128, NCH * TT]) for b in range(2)]
        rden = [sb(nc, sctx, f"{tag}B_rden{b}", [128, TT]) for b in range(2)]
        msg = [sb(nc, sctx, f"{tag}B_msg{b}", [128, TT]) for b in range(2)]
        ps_den = [ps(nc, sctx, f"{tag}B_pd{b}", [128, TT]) for b in range(2)]
        ps_o = [ps(nc, sctx, f"{tag}B_po{b}", [128, TT]) for b in range(2)]
        for s in range(NSEQ):
            for j in range(NTT):
                i = s * NTT + j
                b = i % 2
                p.add("sync", lambda h, src=fm_slice(io["xn"], s, j, TT), b=b, xt=xt:
                      ld3(h, xt[b], src),
                      reads=[io["xn"]], writes=[xt[b]], dma=True, tag=f"{tag}Bldx{i}")
                p.add("sync", lambda h, src=fm_slice(qdr, s, j, TT), b=b, qt=qt:
                      ld3(h, qt[b], src),
                      reads=[qdr], writes=[qt[b]], dma=True, tag=f"{tag}Bldq{i}")
                qv = qt[b][:].rearrange("p (c t) -> p c t", t=TT)
                smv = smid[b][:].rearrange("p (c t) -> p c t", t=TT)
                xv = xt[b][:].rearrange("p (c t) -> p c t", t=TT)
                for hh in range(NH):
                    pd, po = ps_den[hh % 2], ps_o[hh % 2]
                    p.add("tensor", lambda h, hh=hh, s=s, pd=pd, qv=qv, ks_sb=ks_sb: h.matmul(
                        out=pd[:], lhsT=ks_sb[:, s * 1024 + hh * 128: s * 1024 + (hh + 1) * 128],
                        rhs=qv[:, hh, :], start=True, stop=True),
                        reads=[ks_sb, qt[b]], writes=[pd], tag=f"{tag}Bden{i}.{hh}")
                    p.add("tensor", lambda h, hh=hh, s=s, po=po, qv=qv, kv_sb=kv_sb: h.matmul(
                        out=po[:], lhsT=kv_sb[:, s * 1024 + hh * 128: s * 1024 + (hh + 1) * 128],
                        rhs=qv[:, hh, :], start=True, stop=True),
                        reads=[kv_sb, qt[b]], writes=[po], tag=f"{tag}Bout{i}.{hh}")
                    rd, mg = rden[hh % 2], msg[hh % 2]
                    p.add("vector", lambda h, pd=pd, rd=rd: h.tensor_scalar_add(out=rd[:], in0=pd[:], scalar1=1e-6),
                          reads=[pd], writes=[rd], tag=f"{tag}Beps{i}.{hh}")
                    p.add("vector", lambda h, rd=rd: h.reciprocal(out=rd[:], in_=rd[:]),
                          reads=[rd], writes=[rd], tag=f"{tag}Brec{i}.{hh}")
                    p.add("vector", lambda h, po=po, mg=mg, rd=rd: h.tensor_mul(out=mg[:], in0=po[:], in1=rd[:]),
                          reads=[po, rd], writes=[mg], tag=f"{tag}Bmsg{i}.{hh}")
                    p.add("vector", lambda h, hh=hh, smv=smv, xv=xv, mg=mg: h.tensor_add(
                        out=smv[:, hh, :], in0=mg[:], in1=xv[:, hh, :]),
                        reads=[mg, xt[b]], writes=[smid[b]], tag=f"{tag}Bsm{i}.{hh}")
                p.add("gpsimd", lambda h, sdst=fm_slice(io["smid"], s, j, TT), b=b, smid=smid:
                      st3(h, sdst, smid[b]),
                      reads=[smid[b]], writes=[io["smid"]], dma=True, tag=f"{tag}Bsmst{i}")
        p.full_barrier(f"{tag}_endB1")

    with ExitStack() as sctx:
        load_weight_bf16(p, sctx, staging, arena, 0, io["w1T"], 1024, 4096, f"{tag}w1")
        w1_v = arena[:].rearrange("p (c f) -> p c f", f=4096)
        smt = [sb(nc, sctx, f"{tag}D_sm{b}", [128, NCH * TT]) for b in range(2)]
        x2b = sb(nc, sctx, f"{tag}D_x2b", [128, NCH * TT], BF16)
        hbf = sb(nc, sctx, f"{tag}D_h", [128, 16 * TT], BF16)
        ps_h = [ps(nc, sctx, f"{tag}D_ph{b}", [128, TT]) for b in range(2)]
        ln = LNScratch(nc, sctx, f"{tag}D_ln", TT, ps_h)
        for s in range(NSEQ):
            for j in range(NTT):
                i = s * NTT + j
                b = i % 2
                p.add("sync", lambda h, src=fm_slice(io["smid"], s, j, TT), b=b, smt=smt:
                      ld3(h, smt[b], src),
                      reads=[io["smid"]], writes=[smt[b]], dma=True, tag=f"{tag}Dld{i}")
                emit_ln_fm_tile(p, ln, smt[b], x2b, ones_sb, cc, f"{tag}Dln{i}",
                                gamma=(vecs, 16), beta=(vecs, 24))
                x2bv = x2b[:].rearrange("p (c t) -> p c t", t=TT)
                hv = hbf[:].rearrange("p (c t) -> p c t", t=TT)
                for fhalf in range(2):
                    for fci in range(16):
                        fc = fhalf * 16 + fci
                        pb = ps_h[fc % 2]
                        for dc in range(NCH):
                            p.add("tensor", lambda h, fc=fc, dc=dc, pb=pb, x2bv=x2bv, w1_v=w1_v: h.matmul(
                                out=pb[:], lhsT=w1_v[:, dc, fc * 128:(fc + 1) * 128],
                                rhs=x2bv[:, dc, :], start=(dc == 0), stop=(dc == NCH - 1)),
                                reads=[x2b, arena], writes=[pb], tag=f"{tag}Dh{i}.{fc}.{dc}")
                        p.add("scalar", lambda h, pb=pb, fci=fci, fc=fc, hv=hv, vecs=vecs: h.activation(
                            out=hv[:, fci, :], in_=pb[:], func=AF.Gelu,
                            bias=vecs[:, 32 + fc:33 + fc]),
                            reads=[pb, vecs], writes=[hbf], tag=f"{tag}Dg{i}.{fc}")
                    hdst = io["hbuf"][s].rearrange("(c p) t -> p c t", p=128)[
                        :, fhalf * 16:(fhalf + 1) * 16, j * TT:(j + 1) * TT]
                    p.add("gpsimd", lambda h, hdst=hdst, hbf=hbf: st3(h, hdst, hbf),
                          reads=[hbf], writes=[io["hbuf"]], dma=True, tag=f"{tag}Dhst{i}.{fhalf}")
        p.full_barrier(f"{tag}_endB2")

    with ExitStack() as sctx:
        load_weight_bf16(p, sctx, staging, arena, 0, io["w2T"], 4096, 1024, f"{tag}w2")
        w2_v = arena[:].rearrange("p (c f) -> p c f", f=1024)
        smt = [sb(nc, sctx, f"{tag}C_sm{b}", [128, NCH * TT]) for b in range(2)]
        ht = sb(nc, sctx, f"{tag}C_h", [128, 32 * TT], BF16)
        sout = sb(nc, sctx, f"{tag}C_s", [128, NCH * TT])
        ps_s = [ps(nc, sctx, f"{tag}C_ps{b}", [128, TT]) for b in range(4)]
        for s in range(NSEQ):
            for j in range(NTT):
                i = s * NTT + j
                b = i % 2
                p.add("sync", lambda h, src=fm_slice(io["smid"], s, j, TT), b=b, smt=smt:
                      ld3(h, smt[b], src),
                      reads=[io["smid"]], writes=[smt[b]], dma=True, tag=f"{tag}Cldsm{i}")
                hsrc = io["hbuf"][s].rearrange("(c p) t -> p c t", p=128)[:, :, j * TT:(j + 1) * TT]
                p.add("sync", lambda h, hsrc=hsrc, ht=ht: ld3(h, ht, hsrc),
                      reads=[io["hbuf"]], writes=[ht], dma=True, tag=f"{tag}Cldh{i}")
                hv = ht[:].rearrange("p (c t) -> p c t", t=TT)
                smv = smt[b][:].rearrange("p (c t) -> p c t", t=TT)
                sov = sout[:].rearrange("p (c t) -> p c t", t=TT)
                for dc in range(NCH):
                    pb = ps_s[dc % 4]
                    for fc in range(32):
                        p.add("tensor", lambda h, dc=dc, fc=fc, pb=pb, hv=hv, w2_v=w2_v: h.matmul(
                            out=pb[:], lhsT=w2_v[:, fc, dc * 128:(dc + 1) * 128],
                            rhs=hv[:, fc, :], start=(fc == 0), stop=(fc == 31)),
                            reads=[ht, arena], writes=[pb], tag=f"{tag}Cs{i}.{dc}.{fc}")
                    p.add("vector", lambda h, pb=pb, dc=dc, sov=sov, smv=smv, vecs=vecs: h.scalar_tensor_tensor(
                        out=sov[:, dc, :], in0=pb[:], scalar=vecs[:, 64 + dc:65 + dc],
                        in1=smv[:, dc, :], op0=ALU.add, op1=ALU.add),
                        reads=[pb, vecs, smt[b]], writes=[sout], tag=f"{tag}Cev{i}.{dc}")
                p.add("gpsimd", lambda h, sdst=fm_slice(io["s0"], s, j, TT), sout=sout:
                      st3(h, sdst, sout),
                      reads=[sout], writes=[io["s0"]], dma=True, tag=f"{tag}Cst{i}")
        p.full_barrier(f"{tag}_endC")
    ctx.close()


def emit_l0_moe(p, ctx, io, cc, tag="moe", parts=("R","S","P","F"), npairs=8):
    nc = p.nc
    from contextlib import ExitStack
    ctx = ExitStack()
    TT = 512
    NTT = T // TT
    NTILES = NSEQ * NTT
    E, I = 16, 768
    NIC = I // 128
    BIGM = 1e30

    combT = sb(nc, ctx, f"{tag}_combT", [16, NSEQ * T])
    ident_sb = sb(nc, ctx, f"{tag}_id", [128, 128])
    gb_sb = sb(nc, ctx, f"{tag}_gb", [128, 16])
    consts = {}
    for nm in ("iota16", "iota16mb", "iotaj", "iotajmb"):
        consts[nm] = sb(nc, ctx, f"{tag}_{nm}", [128, 16])
    for nm in ("iota4", "iota4mb"):
        consts[nm] = sb(nc, ctx, f"{tag}_{nm}", [128, 4])
    p.add("sync", lambda h, ident_sb=ident_sb, io=io: h.dma_start(out=ident_sb[:], in_=io["ident"]),
          reads=[io["ident"]], writes=[ident_sb], dma=True, tag=f"{tag}ldid")
    p.add("sync", lambda h, gb_sb=gb_sb, io=io: h.dma_start(out=gb_sb[:], in_=io["gate_bias"].broadcast_to([128, 16])),
          reads=[io["gate_bias"]], writes=[gb_sb], dma=True, tag=f"{tag}ldgb")
    for nm, t_ in consts.items():
        p.add("sync", lambda h, t_=t_, nm=nm, io=io: h.dma_start(out=t_[:], in_=io[nm]),
              reads=[io[nm]], writes=[t_], dma=True, tag=f"{tag}ld{nm}")

    with ExitStack() as sctx:
        gwt = sb(nc, sctx, f"{tag}_gwt", [128, 8 * 16])
        p.add("sync", lambda h, gwt=gwt, io=io: h.dma_start(
            out=gwt[:].rearrange("p (c e) -> p c e", e=16),
            in_=io["gate_wT"].rearrange("(c p) e -> p c e", p=128)),
            reads=[io["gate_wT"]], writes=[gwt], dma=True, tag=f"{tag}ldgw")
        gw_v = gwt[:].rearrange("p (c e) -> p c e", e=16)
        xt = [sb(nc, sctx, f"{tag}R_x{b}", [128, NCH * TT]) for b in range(2)]
        xbf = sb(nc, sctx, f"{tag}R_xbf", [128, NCH * TT], BF16)
        lg = sb(nc, sctx, f"{tag}R_lg", [128, 16])
        sc = sb(nc, sctx, f"{tag}R_sc", [128, 16])
        sbv = sb(nc, sctx, f"{tag}R_sb", [128, 16])
        gs2 = sb(nc, sctx, f"{tag}R_gs2", [128, 16])
        oh = sb(nc, sctx, f"{tag}R_oh", [128, 16])
        idxm = sb(nc, sctx, f"{tag}R_idxm", [128, 16])
        mx = sb(nc, sctx, f"{tag}R_mx", [128, 4])
        mn = sb(nc, sctx, f"{tag}R_mn", [128, 4])
        gsc = sb(nc, sctx, f"{tag}R_gsc", [128, 4])
        goh = sb(nc, sctx, f"{tag}R_goh", [128, 4])
        gohf = sb(nc, sctx, f"{tag}R_gohf", [128, 4])
        gmask = sb(nc, sctx, f"{tag}R_gm", [128, 4])
        col1 = sb(nc, sctx, f"{tag}R_c1", [128, 1])
        col2 = sb(nc, sctx, f"{tag}R_c2", [128, 1])
        sel = sb(nc, sctx, f"{tag}R_sel", [128, 16])
        wts = sb(nc, sctx, f"{tag}R_w", [128, 16])
        comb = sb(nc, sctx, f"{tag}R_comb", [128, 16])
        ps_lg = ps(nc, sctx, f"{tag}R_plg", [128, 16])
        ps_tr = ps(nc, sctx, f"{tag}R_ptr", [16, 128])

        for i in range(NTILES):
            s, j = divmod(i, NTT)
            b = i % 2
            p.add("sync", lambda h, src=fm_slice(io["xn"], s, j, TT), b=b, xt=xt:
                  ld3(h, xt[b], src),
                  reads=[io["xn"]], writes=[xt[b]], dma=True, tag=f"{tag}Rld{i}")
            p.add("vector", lambda h, b=b, xbf=xbf, xt=xt: h.tensor_copy(out=xbf[:], in_=xt[b][:]),
                  reads=[xt[b]], writes=[xbf], tag=f"{tag}Rcast{i}")
            p.add("gpsimd", lambda h, dst=fm_slice(io["xnbf"], s, j, TT), xbf=xbf:
                  st3(h, dst, xbf),
                  reads=[xbf], writes=[io["xnbf"]], dma=True, tag=f"{tag}Rxst{i}")
            xv = xt[b][:].rearrange("p (c t) -> p c t", t=TT)
            for t4 in range(4):
                t0 = t4 * 128
                tglob = s * T + j * TT + t0
                for dc in range(NCH):
                    p.add("tensor", lambda h, dc=dc, t0=t0, xv=xv, gw_v=gw_v, ps_lg=ps_lg: h.matmul(
                        out=ps_lg[:], lhsT=xv[:, dc, t0:t0 + 128], rhs=gw_v[:, dc, :],
                        start=(dc == 0), stop=(dc == NCH - 1)),
                        reads=[xt[b], gwt], writes=[ps_lg], tag=f"{tag}Rmm{i}.{t4}.{dc}")
                p.add("scalar", lambda h, ps_lg=ps_lg, sc=sc: h.activation(out=sc[:], in_=ps_lg[:], func=AF.Sigmoid),
                      reads=[ps_lg], writes=[sc], tag=f"{tag}Rsig{i}.{t4}")
                p.add("vector", lambda h, gb_sb=gb_sb, sbv=sbv, sc=sc: h.tensor_add(out=sbv[:], in0=sc[:], in1=gb_sb[:]),
                      reads=[sc, gb_sb], writes=[sbv], tag=f"{tag}Rsb{i}.{t4}")
                g3 = sbv[:].rearrange("p (g j) -> p g j", j=4)
                p.add("vector", lambda h, g3=g3, mx=mx: h.reduce_max(out=mx[:], in_=g3, axis=AX.X),
                      reads=[sbv], writes=[mx], tag=f"{tag}Rgm1{i}.{t4}")
                p.add("vector", lambda h, g3=g3, mx=mx, oh=oh: h.tensor_tensor(
                    out=oh[:].rearrange("p (g j) -> p g j", j=4), in0=g3,
                    in1=mx[:].unsqueeze(2).broadcast_to([128, 4, 4]), op=ALU.is_equal),
                    reads=[sbv, mx], writes=[oh], tag=f"{tag}Rgoh{i}.{t4}")
                p.add("vector", lambda h, consts=consts, idxm=idxm, oh=oh: h.tensor_mul(out=idxm[:], in0=oh[:], in1=consts["iotajmb"][:]),
                      reads=[oh, consts["iotajmb"]], writes=[idxm], tag=f"{tag}Rgidx2{i}.{t4}")
                p.add("vector", lambda h, idxm=idxm: h.tensor_scalar_add(out=idxm[:], in0=idxm[:], scalar1=65536.0),
                      reads=[idxm], writes=[idxm], tag=f"{tag}Rgidx3{i}.{t4}")
                p.add("vector", lambda h, idxm=idxm, mn=mn: h.tensor_reduce(
                    out=mn[:], in_=idxm[:].rearrange("p (g j) -> p g j", j=4),
                    axis=AX.X, op=ALU.min),
                    reads=[idxm], writes=[mn], tag=f"{tag}Rgmn{i}.{t4}")
                p.add("vector", lambda h, consts=consts, mn=mn, oh=oh: h.tensor_tensor(
                    out=oh[:].rearrange("p (g j) -> p g j", j=4),
                    in0=consts["iotaj"][:].rearrange("p (g j) -> p g j", j=4),
                    in1=mn[:].unsqueeze(2).broadcast_to([128, 4, 4]), op=ALU.is_equal),
                    reads=[consts["iotaj"], mn], writes=[oh], tag=f"{tag}Rgohf{i}.{t4}")
                p.add("vector", lambda h, g3=g3, BIGM=BIGM, gs2=gs2, oh=oh, sbv=sbv: h.scalar_tensor_tensor(
                    out=gs2[:], in0=oh[:], scalar=-BIGM, op0=ALU.mult,
                    in1=sbv[:], op1=ALU.add),
                    reads=[oh, sbv], writes=[gs2], tag=f"{tag}Rgs2{i}.{t4}")
                p.add("vector", lambda h, gs2=gs2, mn=mn: h.reduce_max(
                    out=mn[:], in_=gs2[:].rearrange("p (g j) -> p g j", j=4), axis=AX.X),
                    reads=[gs2], writes=[mn], tag=f"{tag}Rgm2{i}.{t4}")
                p.add("vector", lambda h, gsc=gsc, mn=mn, mx=mx: h.tensor_add(out=gsc[:], in0=mx[:], in1=mn[:]),
                      reads=[mx, mn], writes=[gsc], tag=f"{tag}Rgsc{i}.{t4}")
                p.add("vector", lambda h, col1=col1, gsc=gsc: h.reduce_max(out=col1[:], in_=gsc[:], axis=AX.X),
                      reads=[gsc], writes=[col1], tag=f"{tag}Rt2a{i}.{t4}")
                p.add("vector", lambda h, col1=col1, goh=goh, gsc=gsc: h.tensor_single_scalar(
                    out=goh[:], in_=gsc[:], scalar=col1[:], op=ALU.is_equal),
                    reads=[gsc, col1], writes=[goh], tag=f"{tag}Rt2b{i}.{t4}")
                p.add("vector", lambda h, consts=consts, goh=goh, gohf=gohf: h.tensor_mul(out=gohf[:], in0=goh[:], in1=consts["iota4mb"][:]),
                      reads=[goh, consts["iota4mb"]], writes=[gohf], tag=f"{tag}Rt2c{i}.{t4}")
                p.add("vector", lambda h, gohf=gohf: h.tensor_scalar_add(out=gohf[:], in0=gohf[:], scalar1=65536.0),
                      reads=[gohf], writes=[gohf], tag=f"{tag}Rt2d{i}.{t4}")
                p.add("vector", lambda h, col2=col2, gohf=gohf: h.tensor_reduce(out=col2[:], in_=gohf[:], axis=AX.X, op=ALU.min),
                      reads=[gohf], writes=[col2], tag=f"{tag}Rt2e{i}.{t4}")
                p.add("vector", lambda h, col2=col2, consts=consts, gmask=gmask: h.tensor_single_scalar(
                    out=gmask[:], in_=consts["iota4"][:], scalar=col2[:], op=ALU.is_equal),
                    reads=[consts["iota4"], col2], writes=[gmask], tag=f"{tag}Rt2f{i}.{t4}")
                p.add("vector", lambda h, BIGM=BIGM, gmask=gmask, gs2=gs2, gsc=gsc: h.scalar_tensor_tensor(
                    out=gs2[:, 0:4], in0=gmask[:], scalar=-BIGM, op0=ALU.mult,
                    in1=gsc[:], op1=ALU.add),
                    reads=[gmask, gsc], writes=[gs2], tag=f"{tag}Rt2g{i}.{t4}")
                p.add("vector", lambda h, col1=col1, gs2=gs2: h.reduce_max(out=col1[:], in_=gs2[:, 0:4], axis=AX.X),
                      reads=[gs2], writes=[col1], tag=f"{tag}Rt2h{i}.{t4}")
                p.add("vector", lambda h, col1=col1, goh=goh, gs2=gs2: h.tensor_single_scalar(
                    out=goh[:], in_=gs2[:, 0:4], scalar=col1[:], op=ALU.is_equal),
                    reads=[gs2, col1], writes=[goh], tag=f"{tag}Rt2i{i}.{t4}")
                p.add("vector", lambda h, consts=consts, goh=goh, gohf=gohf: h.tensor_mul(out=gohf[:], in0=goh[:], in1=consts["iota4mb"][:]),
                      reads=[goh, consts["iota4mb"]], writes=[gohf], tag=f"{tag}Rt2j{i}.{t4}")
                p.add("vector", lambda h, gohf=gohf: h.tensor_scalar_add(out=gohf[:], in0=gohf[:], scalar1=65536.0),
                      reads=[gohf], writes=[gohf], tag=f"{tag}Rt2k{i}.{t4}")
                p.add("vector", lambda h, col2=col2, gohf=gohf: h.tensor_reduce(out=col2[:], in_=gohf[:], axis=AX.X, op=ALU.min),
                      reads=[gohf], writes=[col2], tag=f"{tag}Rt2l{i}.{t4}")
                p.add("vector", lambda h, col2=col2, consts=consts, goh=goh: h.tensor_single_scalar(
                    out=goh[:], in_=consts["iota4"][:], scalar=col2[:], op=ALU.is_equal),
                    reads=[consts["iota4"], col2], writes=[goh], tag=f"{tag}Rt2m{i}.{t4}")
                p.add("vector", lambda h, gmask=gmask, goh=goh: h.tensor_add(out=gmask[:], in0=gmask[:], in1=goh[:]),
                      reads=[gmask, goh], writes=[gmask], tag=f"{tag}Rt2n{i}.{t4}")
                p.add("vector", lambda h, BIGM=BIGM, gmask=gmask, oh=oh: h.tensor_scalar(
                    out=oh[:].rearrange("p (g j) -> p g j", j=4),
                    in0=gmask[:].unsqueeze(2).broadcast_to([128, 4, 4]),
                    scalar1=1.0, scalar2=BIGM, op0=ALU.subtract, op1=ALU.mult),
                    reads=[gmask], writes=[oh], tag=f"{tag}Rem{i}.{t4}")
                p.add("vector", lambda h, gs2=gs2, oh=oh, sbv=sbv: h.tensor_add(out=gs2[:], in0=sbv[:], in1=oh[:]),
                      reads=[sbv, oh], writes=[gs2], tag=f"{tag}Rmsk{i}.{t4}")
                p.add("vector", lambda h, sel=sel: h.memset(sel[:], 0.0),
                      reads=[], writes=[sel], tag=f"{tag}Rsel0{i}.{t4}")
                for r in range(4):
                    p.add("vector", lambda h, col1=col1, gs2=gs2: h.reduce_max(out=col1[:], in_=gs2[:], axis=AX.X),
                          reads=[gs2], writes=[col1], tag=f"{tag}Re{r}a{i}.{t4}")
                    p.add("vector", lambda h, col1=col1, gs2=gs2, oh=oh: h.tensor_single_scalar(
                        out=oh[:], in_=gs2[:], scalar=col1[:], op=ALU.is_equal),
                        reads=[gs2, col1], writes=[oh], tag=f"{tag}Re{r}b{i}.{t4}")
                    p.add("vector", lambda h, consts=consts, idxm=idxm, oh=oh: h.tensor_mul(out=idxm[:], in0=oh[:], in1=consts["iota16mb"][:]),
                          reads=[oh, consts["iota16mb"]], writes=[idxm], tag=f"{tag}Re{r}c{i}.{t4}")
                    p.add("vector", lambda h, idxm=idxm: h.tensor_scalar_add(out=idxm[:], in0=idxm[:], scalar1=65536.0),
                          reads=[idxm], writes=[idxm], tag=f"{tag}Re{r}d{i}.{t4}")
                    p.add("vector", lambda h, col2=col2, idxm=idxm: h.tensor_reduce(out=col2[:], in_=idxm[:], axis=AX.X, op=ALU.min),
                          reads=[idxm], writes=[col2], tag=f"{tag}Re{r}e{i}.{t4}")
                    p.add("vector", lambda h, col2=col2, consts=consts, oh=oh: h.tensor_single_scalar(
                        out=oh[:], in_=consts["iota16"][:], scalar=col2[:], op=ALU.is_equal),
                        reads=[consts["iota16"], col2], writes=[oh], tag=f"{tag}Re{r}f{i}.{t4}")
                    p.add("vector", lambda h, oh=oh, sel=sel: h.tensor_add(out=sel[:], in0=sel[:], in1=oh[:]),
                          reads=[sel, oh], writes=[sel], tag=f"{tag}Re{r}g{i}.{t4}")
                    if r < 3:
                        p.add("vector", lambda h, BIGM=BIGM, gs2=gs2, oh=oh: h.scalar_tensor_tensor(
                            out=gs2[:], in0=oh[:], scalar=-BIGM, op0=ALU.mult,
                            in1=gs2[:], op1=ALU.add),
                            reads=[oh, gs2], writes=[gs2], tag=f"{tag}Re{r}h{i}.{t4}")
                p.add("vector", lambda h, sc=sc, sel=sel, wts=wts: h.tensor_mul(out=wts[:], in0=sc[:], in1=sel[:]),
                      reads=[sc, sel], writes=[wts], tag=f"{tag}Rw{i}.{t4}")
                p.add("vector", lambda h, col1=col1, wts=wts: h.reduce_sum(out=col1[:], in_=wts[:], axis=AX.X),
                      reads=[wts], writes=[col1], tag=f"{tag}Rws{i}.{t4}")
                p.add("vector", lambda h, col1=col1: h.tensor_scalar_add(out=col1[:], in0=col1[:], scalar1=1e-20),
                      reads=[col1], writes=[col1], tag=f"{tag}Rwe{i}.{t4}")
                p.add("vector", lambda h, col1=col1, col2=col2: h.reciprocal(out=col2[:], in_=col1[:]),
                      reads=[col1], writes=[col2], tag=f"{tag}Rwr{i}.{t4}")
                p.add("vector", lambda h, col2=col2, comb=comb, wts=wts: h.tensor_scalar(
                    out=comb[:], in0=wts[:], scalar1=col2[:], scalar2=2.5,
                    op0=ALU.mult, op1=ALU.mult),
                    reads=[wts, col2], writes=[comb], tag=f"{tag}Rcb{i}.{t4}")
                p.add("tensor", lambda h, comb=comb, ident_sb=ident_sb, ps_tr=ps_tr: h.transpose(out=ps_tr[:], in_=comb[:], identity=ident_sb[:]),
                      reads=[comb, ident_sb], writes=[ps_tr], tag=f"{tag}Rtr{i}.{t4}")
                p.add("scalar", lambda h, tglob=tglob, combT=combT, ps_tr=ps_tr: h.activation(
                    out=combT[:, tglob:tglob + 128], in_=ps_tr[:], func=AF.Copy),
                    reads=[ps_tr], writes=[combT], tag=f"{tag}Rtrev{i}.{t4}")
        p.full_barrier(f"{tag}_endR")

    E_COLS = 6144
    if "S" not in parts:
        return
    with ExitStack() as sctx:
        staging = [sb(nc, sctx, f"{tag}S_stg{b}", [128, 1024]) for b in range(2)]
        arena = sb(nc, sctx, f"{tag}S_arena", [128, 3 * E_COLS], BF16)
        load_weight_bf16(p, sctx, staging, arena, 0, io["ws1T"], 1024, 768, f"{tag}ws1")
        load_weight_bf16(p, sctx, staging, arena, E_COLS, io["ws3T"], 1024, 768, f"{tag}ws3")
        load_weight_bf16(p, sctx, staging, arena, 2 * E_COLS, io["ws2T"], 768, 1024, f"{tag}ws2")
        w1v = arena[:, 0:E_COLS].rearrange("p (c f) -> p c f", f=768)
        w3v = arena[:, E_COLS:2 * E_COLS].rearrange("p (c f) -> p c f", f=768)
        w2v = arena[:, 2 * E_COLS:3 * E_COLS].rearrange("p (c f) -> p c f", f=1024)
        xbt = [sb(nc, sctx, f"{tag}S_x{b}", [128, NCH * TT], BF16) for b in range(2)]
        hst = sb(nc, sctx, f"{tag}S_hs", [128, NIC * TT], BF16)
        sg = [sb(nc, sctx, f"{tag}S_sg{b}", [128, TT]) for b in range(2)]
        yt = [sb(nc, sctx, f"{tag}S_y{b}", [128, NCH * TT]) for b in range(2)]
        ps_g = [ps(nc, sctx, f"{tag}S_pg{b}", [128, TT]) for b in range(2)]
        ps_u = [ps(nc, sctx, f"{tag}S_pu{b}", [128, TT]) for b in range(2)]
        ps_y = [ps(nc, sctx, f"{tag}S_py{b}", [128, TT]) for b in range(4)]
        for i in range(NTILES):
            s, j = divmod(i, NTT)
            b = i % 2
            p.add("sync", lambda h, src=fm_slice(io["xnbf"], s, j, TT), b=b, xbt=xbt:
                  ld3(h, xbt[b], src),
                  reads=[io["xnbf"]], writes=[xbt[b]], dma=True, tag=f"{tag}Sld{i}")
            xv = xbt[b][:].rearrange("p (c t) -> p c t", t=TT)
            hv = hst[:].rearrange("p (c t) -> p c t", t=TT)
            yv = yt[b][:].rearrange("p (c t) -> p c t", t=TT)
            for ic in range(NIC):
                pg, pu = ps_g[ic % 2], ps_u[ic % 2]
                for dc in range(NCH):
                    p.add("tensor", lambda h, ic=ic, dc=dc, pg=pg, xv=xv, w1v=w1v: h.matmul(
                        out=pg[:], lhsT=w1v[:, dc, ic * 128:(ic + 1) * 128],
                        rhs=xv[:, dc, :], start=(dc == 0), stop=(dc == NCH - 1)),
                        reads=[xbt[b], arena], writes=[pg], tag=f"{tag}Sg{i}.{ic}.{dc}")
                for dc in range(NCH):
                    p.add("tensor", lambda h, ic=ic, dc=dc, pu=pu, xv=xv, w3v=w3v: h.matmul(
                        out=pu[:], lhsT=w3v[:, dc, ic * 128:(ic + 1) * 128],
                        rhs=xv[:, dc, :], start=(dc == 0), stop=(dc == NCH - 1)),
                        reads=[xbt[b], arena], writes=[pu], tag=f"{tag}Su{i}.{ic}.{dc}")
                sgc = sg[ic % 2]
                p.add("scalar", lambda h, pg=pg, sgc=sgc: h.activation(out=sgc[:], in_=pg[:], func=AF.Silu),
                      reads=[pg], writes=[sgc], tag=f"{tag}Ssil{i}.{ic}")
                p.add("vector", lambda h, pu=pu, ic=ic, hv=hv, sgc=sgc: h.tensor_mul(
                    out=hv[:, ic, :], in0=pu[:], in1=sgc[:]),
                    reads=[pu, sgc], writes=[hst], tag=f"{tag}Shs{i}.{ic}")
            for dc in range(NCH):
                py = ps_y[dc % 4]
                for ic in range(NIC):
                    p.add("tensor", lambda h, dc=dc, ic=ic, py=py, hv=hv, NIC=NIC, w2v=w2v: h.matmul(
                        out=py[:], lhsT=w2v[:, ic, dc * 128:(dc + 1) * 128],
                        rhs=hv[:, ic, :], start=(ic == 0), stop=(ic == NIC - 1)),
                        reads=[hst, arena], writes=[py], tag=f"{tag}Sy{i}.{dc}.{ic}")
                p.add("vector", lambda h, py=py, dc=dc, yv=yv: h.tensor_copy(
                    out=yv[:, dc, :], in_=py[:]),
                    reads=[py], writes=[yt[b]], tag=f"{tag}Syev{i}.{dc}")
            p.add("gpsimd", lambda h, dst=fm_slice(io["ybuf"], s, j, TT), b=b, yt=yt:
                  st3(h, dst, yt[b]),
                  reads=[yt[b]], writes=[io["ybuf"]], dma=True, tag=f"{tag}Syst{i}")
        p.full_barrier(f"{tag}_endS")

    if "P" not in parts:
        return
    sel16 = sb(nc, ctx, f"{tag}_sel16", [16, 16 * 128])
    p.add("sync", lambda h, io=io, sel16=sel16: h.dma_start(out=sel16[:], in_=io["sel16"]),
          reads=[io["sel16"]], writes=[sel16], dma=True, tag=f"{tag}ldsel")
    for pr in range(npairs):
        with ExitStack() as sctx:
            staging = [sb(nc, sctx, f"{tag}P{pr}_stg{b}", [128, 1024]) for b in range(2)]
            arena = sb(nc, sctx, f"{tag}P{pr}_arena", [128, 6 * E_COLS], BF16)
            for e01 in range(2):
                e = pr * 2 + e01
                off = e01 * 3 * E_COLS
                load_weight_bf16(p, sctx, staging, arena, off, io["w1Tall"][e], 1024, 768, f"{tag}p{pr}w1e{e01}")
                load_weight_bf16(p, sctx, staging, arena, off + E_COLS, io["w3Tall"][e], 1024, 768, f"{tag}p{pr}w3e{e01}")
                load_weight_bf16(p, sctx, staging, arena, off + 2 * E_COLS, io["w2Tall"][e], 768, 1024, f"{tag}p{pr}w2e{e01}")
            xbt = [sb(nc, sctx, f"{tag}P{pr}_x{b}", [128, NCH * TT], BF16) for b in range(2)]
            yt = [sb(nc, sctx, f"{tag}P{pr}_y{b}", [128, NCH * TT]) for b in range(2)]
            hst2 = [sb(nc, sctx, f"{tag}P{pr}_hs{e}", [128, NIC * TT], BF16) for e in range(2)]
            cbb = [sb(nc, sctx, f"{tag}P{pr}_cb{e}", [128, TT]) for e in range(2)]
            sg = [sb(nc, sctx, f"{tag}P{pr}_sg{b}", [128, TT]) for b in range(2)]
            tmve = [sb(nc, sctx, f"{tag}P{pr}_tmv{b}", [128, TT]) for b in range(2)]
            ps_g = [ps(nc, sctx, f"{tag}P{pr}_pg{b}", [128, TT]) for b in range(2)]
            ps_u = [ps(nc, sctx, f"{tag}P{pr}_pu{b}", [128, TT]) for b in range(2)]
            ps_y = [ps(nc, sctx, f"{tag}P{pr}_py{b}", [128, TT]) for b in range(2)]
            ps_cb = ps(nc, sctx, f"{tag}P{pr}_pcb", [128, TT])
            for i in range(NTILES):
                s, j = divmod(i, NTT)
                b = i % 2
                tcol = s * T + j * TT
                p.add("sync", lambda h, src=fm_slice(io["xnbf"], s, j, TT), b=b, xbt=xbt:
                      ld3(h, xbt[b], src),
                      reads=[io["xnbf"]], writes=[xbt[b]], dma=True, tag=f"{tag}P{pr}ldx{i}")
                p.add("sync", lambda h, src=fm_slice(io["ybuf"], s, j, TT), b=b, yt=yt:
                      ld3(h, yt[b], src),
                      reads=[io["ybuf"]], writes=[yt[b]], dma=True, tag=f"{tag}P{pr}ldy{i}")
                xv = xbt[b][:].rearrange("p (c t) -> p c t", t=TT)
                yv = yt[b][:].rearrange("p (c t) -> p c t", t=TT)
                for e01 in range(2):
                    e = pr * 2 + e01
                    p.add("tensor", lambda h, e=e, tcol=tcol, TT=TT, combT=combT, ps_cb=ps_cb, sel16=sel16: h.matmul(
                        out=ps_cb[:], lhsT=sel16[:, e * 128:(e + 1) * 128],
                        rhs=combT[:, tcol:tcol + TT], start=True, stop=True),
                        reads=[sel16, combT], writes=[ps_cb], tag=f"{tag}P{pr}cb{i}.{e01}")
                    p.add("scalar", lambda h, e01=e01, cbb=cbb, ps_cb=ps_cb: h.activation(
                        out=cbb[e01][:], in_=ps_cb[:], func=AF.Copy),
                        reads=[ps_cb], writes=[cbb[e01]], tag=f"{tag}P{pr}cbe{i}.{e01}")
                for e01 in range(2):
                    woff = e01 * 3 * E_COLS
                    w1v = arena[:, woff:woff + E_COLS].rearrange("p (c f) -> p c f", f=768)
                    w3v = arena[:, woff + E_COLS:woff + 2 * E_COLS].rearrange("p (c f) -> p c f", f=768)
                    hv = hst2[e01][:].rearrange("p (c t) -> p c t", t=TT)
                    for ic in range(NIC):
                        pg, pu = ps_g[ic % 2], ps_u[ic % 2]
                        for dc in range(NCH):
                            p.add("tensor", lambda h, ic=ic, dc=dc, pg=pg, xv=xv, w1v=w1v: h.matmul(
                                out=pg[:], lhsT=w1v[:, dc, ic * 128:(ic + 1) * 128],
                                rhs=xv[:, dc, :], start=(dc == 0), stop=(dc == NCH - 1)),
                                reads=[xbt[b], arena], writes=[pg], tag=f"{tag}P{pr}g{i}.{e01}.{ic}.{dc}")
                        for dc in range(NCH):
                            p.add("tensor", lambda h, ic=ic, dc=dc, pu=pu, xv=xv, w3v=w3v: h.matmul(
                                out=pu[:], lhsT=w3v[:, dc, ic * 128:(ic + 1) * 128],
                                rhs=xv[:, dc, :], start=(dc == 0), stop=(dc == NCH - 1)),
                                reads=[xbt[b], arena], writes=[pu], tag=f"{tag}P{pr}u{i}.{e01}.{ic}.{dc}")
                        sgc, tmc = sg[ic % 2], tmve[ic % 2]
                        p.add("scalar", lambda h, pg=pg, sgc=sgc: h.activation(out=sgc[:], in_=pg[:], func=AF.Silu),
                              reads=[pg], writes=[sgc], tag=f"{tag}P{pr}sil{i}.{e01}.{ic}")
                        p.add("vector", lambda h, pu=pu, sgc=sgc, tmc=tmc: h.tensor_mul(out=tmc[:], in0=pu[:], in1=sgc[:]),
                              reads=[pu, sgc], writes=[tmc], tag=f"{tag}P{pr}hs1{i}.{e01}.{ic}")
                        p.add("vector", lambda h, e01=e01, ic=ic, hv=hv, cbb=cbb, tmc=tmc: h.tensor_mul(
                            out=hv[:, ic, :], in0=tmc[:], in1=cbb[e01][:]),
                            reads=[tmc, cbb[e01]], writes=[hst2[e01]], tag=f"{tag}P{pr}hs2{i}.{e01}.{ic}")
                if i == 0 and pr == 0 and "dbg1" in io:
                    p.add("gpsimd", lambda h, hst2=hst2, io=io: h.dma_start(out=io["dbg1"], in_=hst2[0][:]),
                          reads=[hst2[0]], writes=[io["dbg1"]], dma=True, tag="dbg1")
                    p.add("gpsimd", lambda h, cbb=cbb, io=io: h.dma_start(out=io["dbg2"], in_=cbb[0][:]),
                          reads=[cbb[0]], writes=[io["dbg2"]], dma=True, tag="dbg2")
                    p.add("gpsimd", lambda h, E_COLS=E_COLS, arena=arena, io=io: h.dma_start(out=io["dbg3"], in_=arena[:, 2 * E_COLS:3 * E_COLS]),
                          reads=[arena], writes=[io["dbg3"]], dma=True, tag="dbg3")
                for dc in range(NCH):
                    py = ps_y[dc % 2]
                    for e01 in range(2):
                        woff = e01 * 3 * E_COLS
                        w2v = arena[:, woff + 2 * E_COLS:woff + 3 * E_COLS].rearrange("p (c f) -> p c f", f=1024)
                        hv = hst2[e01][:].rearrange("p (c t) -> p c t", t=TT)
                        for ic in range(NIC):
                            p.add("tensor", lambda h, dc=dc, ic=ic, e01=e01, py=py, hv=hv, w2v=w2v, NIC=NIC: h.matmul(
                                out=py[:], lhsT=w2v[:, ic, dc * 128:(dc + 1) * 128],
                                rhs=hv[:, ic, :], start=(e01 == 0 and ic == 0),
                                stop=(e01 == 1 and ic == NIC - 1)),
                                reads=[hst2[e01], arena], writes=[py], tag=f"{tag}P{pr}y{i}.{dc}.{e01}.{ic}")
                    p.add("vector", lambda h, py=py, dc=dc, yv=yv: h.tensor_add(
                        out=yv[:, dc, :], in0=py[:], in1=yv[:, dc, :]),
                        reads=[py, yt[b]], writes=[yt[b]], tag=f"{tag}P{pr}yev{i}.{dc}")
                p.add("gpsimd", lambda h, dst=fm_slice(io["ybuf"], s, j, TT), b=b, yt=yt:
                      st3(h, dst, yt[b]),
                      reads=[yt[b]], writes=[io["ybuf"]], dma=True, tag=f"{tag}P{pr}yst{i}")
            p.full_barrier(f"{tag}_endP{pr}")

    if "F" not in parts:
        return
    with ExitStack() as sctx:
        xt = [sb(nc, sctx, f"{tag}F_x{b}", [128, NCH * TT]) for b in range(2)]
        st = [sb(nc, sctx, f"{tag}F_s{b}", [128, NCH * TT]) for b in range(2)]
        yt = [sb(nc, sctx, f"{tag}F_y{b}", [128, NCH * TT]) for b in range(2)]
        gt = sb(nc, sctx, f"{tag}F_g", [128, NCH * TT])
        for i in range(NTILES):
            s, j = divmod(i, NTT)
            b = i % 2
            p.add("sync", lambda h, src=fm_slice(io["x"], s, j, TT), b=b, xt=xt:
                  ld3(h, xt[b], src),
                  reads=[io["x"]], writes=[xt[b]], dma=True, tag=f"{tag}Fldx{i}")
            p.add("sync", lambda h, src=fm_slice(io["s0"], s, j, TT), b=b, st=st:
                  ld3(h, st[b], src),
                  reads=[io["s0"]], writes=[st[b]], dma=True, tag=f"{tag}Flds{i}")
            p.add("sync", lambda h, src=fm_slice(io["ybuf"], s, j, TT), b=b, yt=yt:
                  ld3(h, yt[b], src),
                  reads=[io["ybuf"]], writes=[yt[b]], dma=True, tag=f"{tag}Fldy{i}")
            p.add("scalar", lambda h, b=b, gt=gt, yt=yt: h.activation(out=gt[:], in_=yt[b][:], func=AF.Silu),
                  reads=[yt[b]], writes=[gt], tag=f"{tag}Fsil{i}")
            p.add("vector", lambda h, b=b, gt=gt, st=st: h.tensor_mul(out=gt[:], in0=gt[:], in1=st[b][:]),
                  reads=[gt, st[b]], writes=[gt], tag=f"{tag}Fmul{i}")
            p.add("vector", lambda h, b=b, gt=gt, xt=xt: h.tensor_add(out=xt[b][:], in0=xt[b][:], in1=gt[:]),
                  reads=[xt[b], gt], writes=[xt[b]], tag=f"{tag}Fadd{i}")
            p.add("gpsimd", lambda h, dst=fm_slice(io["x1"], s, j, TT), b=b, xt=xt:
                  st3(h, dst, xt[b]),
                  reads=[xt[b]], writes=[io["x1"]], dma=True, tag=f"{tag}Fst{i}")
        p.full_barrier(f"{tag}_endF")
    ctx.close()


def emit_l1_mamba(p, ctx, io, cc, tag="mb"):
    nc = p.nc
    from contextlib import ExitStack
    ctx = ExitStack()
    TT = 512
    NTT = T // TT
    DI = 2048
    NDC = DI // 128

    mv = sb(nc, ctx, f"{tag}_mv", [128, 96])
    acols = sb(nc, ctx, f"{tag}_ac", [128, 256])
    sel32 = sb(nc, ctx, f"{tag}_sel32", [32, 32 * 128])
    p.add("sync", lambda h, mv=mv, io=io: h.dma_start(out=mv[:], in_=io["mvecs"]),
          reads=[io["mvecs"]], writes=[mv], dma=True, tag=f"{tag}ldmv")
    p.add("sync", lambda h, acols=acols, io=io: h.dma_start(out=acols[:], in_=io["alog"]),
          reads=[io["alog"]], writes=[acols], dma=True, tag=f"{tag}ldal")
    p.add("sync", lambda h, sel32=sel32, io=io: h.dma_start(out=sel32[:], in_=io["sel32"]),
          reads=[io["sel32"]], writes=[sel32], dma=True, tag=f"{tag}ldsel")
    p.add("scalar", lambda h, acols=acols: h.activation(out=acols[:], in_=acols[:], func=AF.Exp),
          reads=[acols], writes=[acols], tag=f"{tag}aexp")
    p.add("vector", lambda h, acols=acols: h.tensor_scalar_mul(out=acols[:], in0=acols[:], scalar1=-1.0),
          reads=[acols], writes=[acols], tag=f"{tag}aneg")

    with ExitStack() as sctx:
        staging = [sb(nc, sctx, f"{tag}1_stg{b}", [128, 1024]) for b in range(2)]
        arena = sb(nc, sctx, f"{tag}1_arena", [128, 8 * 4096], BF16)
        load_weight_bf16(p, sctx, staging, arena, 0, io["in_proj"], 1024, 4096, f"{tag}wip")
        wv = arena[:].rearrange("p (c f) -> p c f", f=4096)
        xt = [sb(nc, sctx, f"{tag}1_x{b}", [128, NCH * TT]) for b in range(2)]
        xbf = sb(nc, sctx, f"{tag}1_xbf", [128, NCH * TT], BF16)
        ur = [sb(nc, sctx, f"{tag}1_ur", [128, NDC * TT])] * 2
        sg = [sb(nc, sctx, f"{tag}1_sg", [128, NDC * TT], BF16)] * 2
        pp = [ps(nc, sctx, f"{tag}1_p{b}", [128, TT]) for b in range(4)]
        for i in range(NSEQ * NTT):
            s, j = divmod(i, NTT)
            b = i % 2
            p.add("sync", lambda h, src=fm_slice(io["xn1"], s, j, TT), b=b, xt=xt: ld3(h, xt[b], src),
                  reads=[io["xn1"]], writes=[xt[b]], dma=True, tag=f"{tag}1ld{i}")
            p.add("vector", lambda h, b=b, xt=xt, xbf=xbf: h.tensor_copy(out=xbf[:], in_=xt[b][:]),
                  reads=[xt[b]], writes=[xbf], tag=f"{tag}1cast{i}")
            for fc in range(32):
                pb = pp[fc % 4]
                for dc in range(NCH):
                    p.add("tensor", lambda h, fc=fc, dc=dc, pb=pb, xbf=xbf, wv=wv, TT=TT: h.matmul(
                        out=pb[:], lhsT=wv[:, dc, fc * 128:(fc + 1) * 128],
                        rhs=xbf[:].rearrange("p (c t) -> p c t", t=TT)[:, dc, :],
                        start=(dc == 0), stop=(dc == NCH - 1)),
                        reads=[xbf, arena], writes=[pb], tag=f"{tag}1mm{i}.{fc}.{dc}")
                if fc < 16:
                    p.add("vector", lambda h, pb=pb, fc=fc, b=b, ur=ur, TT=TT: h.tensor_copy(
                        out=ur[b][:].rearrange("p (c t) -> p c t", t=TT)[:, fc, :], in_=pb[:]),
                        reads=[pb], writes=[ur[b]], tag=f"{tag}1ev{i}.{fc}")
                else:
                    p.add("scalar", lambda h, pb=pb, fc=fc, b=b, sg=sg, TT=TT: h.activation(
                        out=sg[b][:].rearrange("p (c t) -> p c t", t=TT)[:, fc - 16, :],
                        in_=pb[:], func=AF.Silu),
                        reads=[pb], writes=[sg[b]], tag=f"{tag}1sil{i}.{fc}")
            p.add("gpsimd", lambda h, dst=fm_slice(io["uraw"], s, j, TT, nch=NDC), b=b, ur=ur: st3(h, dst, ur[b]),
                  reads=[ur[b]], writes=[io["uraw"]], dma=True, tag=f"{tag}1stu{i}")
            p.add("gpsimd", lambda h, dst=fm_slice(io["silg"], s, j, TT, nch=NDC), b=b, sg=sg: st3(h, dst, sg[b]),
                  reads=[sg[b]], writes=[io["silg"]], dma=True, tag=f"{tag}1stg{i}")
        p.full_barrier(f"{tag}_end1")

    with ExitStack() as sctx:
        staging = [sb(nc, sctx, f"{tag}2_stg{b}", [128, 1024]) for b in range(2)]
        arena = sb(nc, sctx, f"{tag}2_arena", [128, 16 * 2080], BF16)
        load_weight_bf16(p, sctx, staging, arena, 0, io["x_proj"], 2048, 2080, f"{tag}wxp")
        wxv = arena[:].rearrange("p (c f) -> p c f", f=2080)
        uh = sb(nc, sctx, f"{tag}2_u", [128, NDC * (TT + 2)])
        ucv = [sb(nc, sctx, f"{tag}2_ucv{b}", [128, TT]) for b in range(2)]
        ucv2 = [sb(nc, sctx, f"{tag}2_ucv2{b}", [128, TT]) for b in range(2)]
        usil = sb(nc, sctx, f"{tag}2_usil", [128, NDC * TT])
        ubf = sb(nc, sctx, f"{tag}2_ubf", [128, NDC * TT], BF16)
        dpre = sb(nc, sctx, f"{tag}2_dpre", [128, NDC * TT], BF16)
        bct = sb(nc, sctx, f"{tag}2_bc", [32, TT])
        pp = [ps(nc, sctx, f"{tag}2_p{b}", [128, TT]) for b in range(4)]
        ppbc = ps(nc, sctx, f"{tag}2_pbc", [32, TT])
        for i in range(NSEQ * NTT):
            s, j = divmod(i, NTT)
            t0 = j * TT
            lo = max(t0 - 1, 0)
            hi = min(t0 + TT + 1, T)
            src = io["uraw"][s].rearrange("(c p) t -> p c t", p=128)[:, :, lo:hi]
            p.add("vector", lambda h, uh=uh: h.memset(uh[:], 0.0),
                  reads=[], writes=[uh], tag=f"{tag}2z{i}")
            off = 1 if t0 == 0 else 0
            p.add("sync", lambda h, src=src, uh=uh, off=off, lo=lo, hi=hi, TT=TT: h.dma_start(
                out=uh[:].rearrange("p (c t) -> p c t", t=TT + 2)[:, :, off:off + hi - lo], in_=src),
                reads=[io["uraw"]], writes=[uh], dma=True, tag=f"{tag}2ld{i}")
            uhv = uh[:].rearrange("p (c t) -> p c t", t=TT + 2)
            usv = usil[:].rearrange("p (c t) -> p c t", t=TT)
            for dc in range(NDC):
                uc, uc2 = ucv[dc % 2], ucv2[dc % 2]
                p.add("vector", lambda h, dc=dc, uhv=uhv, uc=uc, mv=mv, TT=TT, uh=uh: h.tensor_scalar_mul(
                    out=uc[:], in0=uhv[:, dc, 0:TT], scalar1=mv[:, dc:dc + 1]),
                    reads=[uh, mv], writes=[uc], tag=f"{tag}2c0{i}.{dc}")
                p.add("vector", lambda h, dc=dc, uhv=uhv, uc=uc, uc2=uc2, mv=mv, TT=TT, uh=uh: h.scalar_tensor_tensor(
                    out=uc2[:], in0=uhv[:, dc, 1:TT + 1], scalar=mv[:, 16 + dc:17 + dc],
                    in1=uc[:], op0=ALU.mult, op1=ALU.add),
                    reads=[uh, uc, mv], writes=[uc2], tag=f"{tag}2c1{i}.{dc}")
                p.add("vector", lambda h, dc=dc, uhv=uhv, uc=uc, uc2=uc2, mv=mv, TT=TT, uh=uh: h.scalar_tensor_tensor(
                    out=uc[:], in0=uhv[:, dc, 2:TT + 2], scalar=mv[:, 32 + dc:33 + dc],
                    in1=uc2[:], op0=ALU.mult, op1=ALU.add),
                    reads=[uh, uc2, mv], writes=[uc], tag=f"{tag}2c2{i}.{dc}")
                p.add("scalar", lambda h, dc=dc, uc=uc, usv=usv, mv=mv, usil=usil: h.activation(
                    out=usv[:, dc, :], in_=uc[:], func=AF.Silu, bias=mv[:, 48 + dc:49 + dc]),
                    reads=[uc, mv], writes=[usil], tag=f"{tag}2sil{i}.{dc}")
            p.add("vector", lambda h, usil=usil, ubf=ubf: h.tensor_copy(out=ubf[:], in_=usil[:]),
                  reads=[usil], writes=[ubf], tag=f"{tag}2cast{i}")
            p.add("gpsimd", lambda h, dst=fm_slice(io["usil"], s, j, TT), usil=usil: st3(h, dst, usil),
                  reads=[usil], writes=[io["usil"]], dma=True, tag=f"{tag}2stu{i}")
            ubv = ubf[:].rearrange("p (c t) -> p c t", t=TT)
            for fc in range(NDC):
                pb = pp[fc % 4]
                for dc in range(NDC):
                    p.add("tensor", lambda h, fc=fc, dc=dc, pb=pb, ubv=ubv, wxv=wxv, ubf=ubf, arena=arena, NDC=NDC: h.matmul(
                        out=pb[:], lhsT=wxv[:, dc, fc * 128:(fc + 1) * 128],
                        rhs=ubv[:, dc, :], start=(dc == 0), stop=(dc == NDC - 1)),
                        reads=[ubf, arena], writes=[pb], tag=f"{tag}2xp{i}.{fc}.{dc}")
                p.add("vector", lambda h, pb=pb, fc=fc, dpre=dpre, TT=TT: h.tensor_copy(
                    out=dpre[:].rearrange("p (c t) -> p c t", t=TT)[:, fc, :], in_=pb[:]),
                    reads=[pb], writes=[dpre], tag=f"{tag}2xpe{i}.{fc}")
            for dc in range(NDC):
                p.add("tensor", lambda h, dc=dc, ubv=ubv, wxv=wxv, ppbc=ppbc, ubf=ubf, arena=arena, NDC=NDC: h.matmul(
                    out=ppbc[:], lhsT=wxv[:, dc, 2048:2080],
                    rhs=ubv[:, dc, :], start=(dc == 0), stop=(dc == NDC - 1)),
                    reads=[ubf, arena], writes=[ppbc], tag=f"{tag}2bc{i}.{dc}")
            p.add("vector", lambda h, ppbc=ppbc, bct=bct: h.tensor_copy(out=bct[:], in_=ppbc[:]),
                  reads=[ppbc], writes=[bct], tag=f"{tag}2bce{i}")
            p.add("gpsimd", lambda h, dst=io["bc"][s][:, t0:t0 + TT], bct=bct: h.dma_start(out=dst, in_=bct[:]),
                  reads=[bct], writes=[io["bc"]], dma=True, tag=f"{tag}2bcst{i}")
            p.add("gpsimd", lambda h, dst=fm_slice(io["dpre"], s, j, TT), dpre=dpre: st3(h, dst, dpre),
                  reads=[dpre], writes=[io["dpre"]], dma=True, tag=f"{tag}2stdp{i}")
        p.full_barrier(f"{tag}_end2a")

    with ExitStack() as sctx:
        staging = [sb(nc, sctx, f"{tag}2b_stg{b}", [128, 1024]) for b in range(2)]
        arena = sb(nc, sctx, f"{tag}2b_arena", [128, 16 * 2048], BF16)
        load_weight_bf16(p, sctx, staging, arena, 0, io["dt_proj"], 2048, 2048, f"{tag}wdt")
        wdv = arena[:].rearrange("p (c f) -> p c f", f=2048)
        dpt = [sb(nc, sctx, f"{tag}2b_dp{b}", [128, NDC * TT], BF16) for b in range(2)]
        dsc = [sb(nc, sctx, f"{tag}2b_dsc{b}", [128, TT]) for b in range(2)]
        dsc2 = [sb(nc, sctx, f"{tag}2b_dsc2{b}", [128, TT]) for b in range(2)]
        delt = sb(nc, sctx, f"{tag}2b_delt", [128, NDC * TT])
        pp = [ps(nc, sctx, f"{tag}2b_p{b}", [128, TT]) for b in range(4)]
        for i in range(NSEQ * NTT):
            s, j = divmod(i, NTT)
            b = i % 2
            p.add("sync", lambda h, src=fm_slice(io["dpre"], s, j, TT), b=b, dpt=dpt: ld3(h, dpt[b], src),
                  reads=[io["dpre"]], writes=[dpt[b]], dma=True, tag=f"{tag}2bld{i}")
            dpv = dpt[b][:].rearrange("p (c t) -> p c t", t=TT)
            dlv = delt[:].rearrange("p (c t) -> p c t", t=TT)
            for fc in range(NDC):
                pb = pp[fc % 4]
                for dc in range(NDC):
                    p.add("tensor", lambda h, fc=fc, dc=dc, pb=pb, dpv=dpv, wdv=wdv, b=b, dpt=dpt, arena=arena, NDC=NDC: h.matmul(
                        out=pb[:], lhsT=wdv[:, dc, fc * 128:(fc + 1) * 128],
                        rhs=dpv[:, dc, :], start=(dc == 0), stop=(dc == NDC - 1)),
                        reads=[dpt[b], arena], writes=[pb], tag=f"{tag}2dt{i}.{fc}.{dc}")
                d1, d2 = dsc[fc % 2], dsc2[fc % 2]
                p.add("scalar", lambda h, pb=pb, fc=fc, d1=d1, mv=mv: h.activation(
                    out=d1[:], in_=pb[:], func=AF.Abs, bias=mv[:, 64 + fc:65 + fc]),
                    reads=[pb, mv], writes=[d1], tag=f"{tag}2sp1{i}.{fc}")
                p.add("scalar", lambda h, d1=d1, d2=d2, cc=cc: h.activation(
                    out=d2[:], in_=d1[:], func=AF.Exp, scale=cc.col(-1.0)),
                    reads=[d1, cc.tile], writes=[d2], tag=f"{tag}2sp2{i}.{fc}")
                p.add("scalar", lambda h, d2=d2, cc=cc: h.activation(
                    out=d2[:], in_=d2[:], func=AF.Ln, bias=cc.col(1.0)),
                    reads=[d2, cc.tile], writes=[d2], tag=f"{tag}2sp3{i}.{fc}")
                p.add("vector", lambda h, pb=pb, fc=fc, d1=d1, mv=mv: h.tensor_scalar_add(
                    out=d1[:], in0=pb[:], scalar1=mv[:, 64 + fc:65 + fc]),
                    reads=[pb, mv], writes=[d1], tag=f"{tag}2sp4{i}.{fc}")
                p.add("vector", lambda h, fc=fc, d1=d1, d2=d2, dlv=dlv, delt=delt: h.scalar_tensor_tensor(
                    out=dlv[:, fc, :], in0=d1[:], scalar=0.0, in1=d2[:],
                    op0=ALU.max, op1=ALU.add),
                    reads=[d1, d2], writes=[delt], tag=f"{tag}2sp5{i}.{fc}")
            p.add("gpsimd", lambda h, dst=fm_slice(io["delt"], s, j, TT), delt=delt: st3(h, dst, delt),
                  reads=[delt], writes=[io["delt"]], dma=True, tag=f"{tag}2std{i}")
        p.full_barrier(f"{tag}_end2b")

    with ExitStack() as sctx:
        dt_t = [sb(nc, sctx, f"{tag}3_d{b}", [128, T]) for b in range(2)]
        u_t = [sb(nc, sctx, f"{tag}3_u{b}", [128, T]) for b in range(2)]
        du_t = [sb(nc, sctx, f"{tag}3_du{b}", [128, T]) for b in range(2)]
        e_t = [sb(nc, sctx, f"{tag}3_e{b}", [128, T]) for b in range(2)]
        x_t = [sb(nc, sctx, f"{tag}3_x{b}", [128, T]) for b in range(2)]
        h_t = [sb(nc, sctx, f"{tag}3_h{b}", [128, T]) for b in range(2)]
        bb_t = [sb(nc, sctx, f"{tag}3_bb{b}", [128, T]) for b in range(2)]
        y_t = [sb(nc, sctx, f"{tag}3_y{b}", [128, T]) for b in range(2)]
        tm_t = sb(nc, sctx, f"{tag}3_tm", [128, T])
        sgt = sb(nc, sctx, f"{tag}3_sg", [128, T], BF16)
        ybf = sb(nc, sctx, f"{tag}3_yb", [128, T], BF16)
        bcs = sb(nc, sctx, f"{tag}3_bcs", [32, T])
        bcsb = sb(nc, sctx, f"{tag}3_bcsb", [32, T], BF16)
        sel32b = sb(nc, sctx, f"{tag}3_sel32b", [32, 32 * 128], BF16)
        prep = ps(nc, sctx, f"{tag}3_pr", [128, 2048])
        prep2 = ps(nc, sctx, f"{tag}3_pr2", [128, 2048])
        p.add("vector", lambda h, sel32=sel32, sel32b=sel32b: h.tensor_copy(
            out=sel32b[:], in_=sel32[:]),
            reads=[sel32], writes=[sel32b], tag=f"{tag}3selc")
        for s in range(NSEQ):
            p.add("sync", lambda h, s=s, bcs=bcs, io=io: h.dma_start(out=bcs[:], in_=io["bc"][s]),
                  reads=[io["bc"]], writes=[bcs], dma=True, tag=f"{tag}3bc{s}")
            p.add("vector", lambda h, bcs=bcs, bcsb=bcsb: h.tensor_copy(out=bcsb[:], in_=bcs[:]),
                  reads=[bcs], writes=[bcsb], tag=f"{tag}3bcc{s}")
            for dblk in range(NDC):
                b = dblk % 2
                src_d = io["delt"][s].rearrange("(c p) t -> p c t", p=128)[:, dblk, :]
                src_u = io["usil"][s].rearrange("(c p) t -> p c t", p=128)[:, dblk, :]
                p.add("sync", lambda h, src_d=src_d, b=b, dt_t=dt_t: h.dma_start(out=dt_t[b][:], in_=src_d),
                      reads=[io["delt"]], writes=[dt_t[b]], dma=True, tag=f"{tag}3ldd{s}.{dblk}")
                p.add("sync", lambda h, src_u=src_u, b=b, u_t=u_t: h.dma_start(out=u_t[b][:], in_=src_u),
                      reads=[io["usil"]], writes=[u_t[b]], dma=True, tag=f"{tag}3ldu{s}.{dblk}")
                p.add("vector", lambda h, b=b, dt_t=dt_t, u_t=u_t, du_t=du_t: h.tensor_mul(
                    out=du_t[b][:], in0=dt_t[b][:], in1=u_t[b][:]),
                    reads=[dt_t[b], u_t[b]], writes=[du_t[b]], tag=f"{tag}3du{s}.{dblk}")
                for ss in range(16):
                    sp = ss % 2
                    bb, ee, xx_, hh_ = bb_t[sp], e_t[sp], x_t[sp], h_t[sp]
                    ydst, ysrc = y_t[ss % 2], y_t[1 - ss % 2]
                    for q in range(4):
                        p.add("tensor", lambda h, ss=ss, q=q, sel32b=sel32b, bcsb=bcsb, prep=prep: h.matmul(
                            out=prep[:, q * 512:(q + 1) * 512], lhsT=sel32b[:, ss * 128:(ss + 1) * 128],
                            rhs=bcsb[:, q * 512:(q + 1) * 512], start=True, stop=True),
                            reads=[sel32b, bcsb], writes=[prep], tag=f"{tag}3rb{s}.{dblk}.{ss}.{q}")
                    p.add("scalar", lambda h, prep=prep, bb=bb: h.activation(out=bb[:], in_=prep[:], func=AF.Copy),
                          reads=[prep], writes=[bb], tag=f"{tag}3rbe{s}.{dblk}.{ss}")
                    for q in range(4):
                        p.add("tensor", lambda h, ss=ss, q=q, sel32b=sel32b, bcsb=bcsb, prep2=prep2: h.matmul(
                            out=prep2[:, q * 512:(q + 1) * 512], lhsT=sel32b[:, (16 + ss) * 128:(17 + ss) * 128],
                            rhs=bcsb[:, q * 512:(q + 1) * 512], start=True, stop=True),
                            reads=[sel32b, bcsb], writes=[prep2], tag=f"{tag}3rc{s}.{dblk}.{ss}.{q}")
                    p.add("scalar", lambda h, b=b, dblk=dblk, ss=ss, dt_t=dt_t, ee=ee, acols=acols: h.activation(
                        out=ee[:], in_=dt_t[b][:], func=AF.Exp,
                        scale=acols[:, dblk * 16 + ss:dblk * 16 + ss + 1]),
                        reads=[dt_t[b], acols], writes=[ee], tag=f"{tag}3e{s}.{dblk}.{ss}")
                    p.add("gpsimd", lambda h, b=b, du_t=du_t, bb=bb, xx_=xx_: h.tensor_mul(
                        out=xx_[:], in0=du_t[b][:], in1=bb[:]),
                        reads=[du_t[b], bb], writes=[xx_], tag=f"{tag}3x{s}.{dblk}.{ss}")
                    p.add("vector", lambda h, ee=ee, xx_=xx_, hh_=hh_: h.tensor_tensor_scan(
                        out=hh_[:], data0=ee[:], data1=xx_[:], initial=0.0,
                        op0=ALU.mult, op1=ALU.add),
                        reads=[ee, xx_], writes=[hh_], tag=f"{tag}3sc{s}.{dblk}.{ss}")
                    if ss == 0:
                        p.add("vector", lambda h, hh_=hh_, prep2=prep2, ydst=ydst: h.tensor_mul(
                            out=ydst[:], in0=hh_[:], in1=prep2[:]),
                            reads=[hh_, prep2], writes=[ydst], tag=f"{tag}3hc{s}.{dblk}.{ss}")
                    else:
                        p.add("vector", lambda h, hh_=hh_, prep2=prep2, tm=tm_t: h.tensor_mul(
                            out=tm[:], in0=hh_[:], in1=prep2[:]),
                            reads=[hh_, prep2], writes=[tm_t], tag=f"{tag}3hc{s}.{dblk}.{ss}")
                        p.add("vector", lambda h, ydst=ydst, ysrc=ysrc, tm=tm_t: h.tensor_add(
                            out=ydst[:], in0=ysrc[:], in1=tm[:]),
                            reads=[ysrc, tm_t], writes=[ydst], tag=f"{tag}3ya{s}.{dblk}.{ss}")
                p.add("vector", lambda h, b=b, dblk=dblk, u_t=u_t, y_t=y_t, mv=mv, tm=tm_t: h.scalar_tensor_tensor(
                    out=tm[:], in0=u_t[b][:], scalar=mv[:, 80 + dblk:81 + dblk],
                    in1=y_t[1][:], op0=ALU.mult, op1=ALU.add),
                    reads=[u_t[b], y_t[1], mv], writes=[tm_t], tag=f"{tag}3ud{s}.{dblk}")
                src_g = io["silg"][s].rearrange("(c p) t -> p c t", p=128)[:, dblk, :]
                p.add("sync", lambda h, src_g=src_g, sgt=sgt: h.dma_start(out=sgt[:], in_=src_g),
                      reads=[io["silg"]], writes=[sgt], dma=True, tag=f"{tag}3ldg{s}.{dblk}")
                p.add("vector", lambda h, sgt=sgt, ybf=ybf, tm=tm_t: h.tensor_mul(
                    out=ybf[:], in0=tm[:], in1=sgt[:]),
                    reads=[tm_t, sgt], writes=[ybf], tag=f"{tag}3yg{s}.{dblk}")
                dst_y = io["ymb"][s].rearrange("(c p) t -> p c t", p=128)[:, dblk, :]
                p.add("gpsimd", lambda h, dst_y=dst_y, ybf=ybf: h.dma_start(out=dst_y, in_=ybf[:]),
                      reads=[ybf], writes=[io["ymb"]], dma=True, tag=f"{tag}3sty{s}.{dblk}")
        p.full_barrier(f"{tag}_end3")

    with ExitStack() as sctx:
        staging = [sb(nc, sctx, f"{tag}5_stg{b}", [128, 1024]) for b in range(2)]
        arena = sb(nc, sctx, f"{tag}5_arena", [128, 16 * 1024], BF16)
        load_weight_bf16(p, sctx, staging, arena, 0, io["out_proj"], 2048, 1024, f"{tag}wop")
        wv = arena[:].rearrange("p (c f) -> p c f", f=1024)
        yt = [sb(nc, sctx, f"{tag}5_y{b}", [128, NDC * TT], BF16) for b in range(2)]
        so = sb(nc, sctx, f"{tag}5_s", [128, NCH * TT])
        pp = [ps(nc, sctx, f"{tag}5_p{b}", [128, TT]) for b in range(4)]
        for i in range(NSEQ * NTT):
            s, j = divmod(i, NTT)
            b = i % 2
            p.add("sync", lambda h, src=fm_slice(io["ymb"], s, j, TT, nch=NDC), b=b, yt=yt: ld3(h, yt[b], src),
                  reads=[io["ymb"]], writes=[yt[b]], dma=True, tag=f"{tag}5ld{i}")
            yv = yt[b][:].rearrange("p (c t) -> p c t", t=TT)
            sov = so[:].rearrange("p (c t) -> p c t", t=TT)
            for fc in range(NCH):
                pb = pp[fc % 4]
                for dc in range(NDC):
                    p.add("tensor", lambda h, fc=fc, dc=dc, pb=pb, yv=yv, wv=wv, NDC=NDC: h.matmul(
                        out=pb[:], lhsT=wv[:, dc, fc * 128:(fc + 1) * 128],
                        rhs=yv[:, dc, :], start=(dc == 0), stop=(dc == NDC - 1)),
                        reads=[yt[b], arena], writes=[pb], tag=f"{tag}5mm{i}.{fc}.{dc}")
                p.add("vector", lambda h, pb=pb, fc=fc, sov=sov: h.tensor_copy(out=sov[:, fc, :], in_=pb[:]),
                      reads=[pb], writes=[so], tag=f"{tag}5ev{i}.{fc}")
            p.add("gpsimd", lambda h, dst=fm_slice(io["s1"], s, j, TT), so=so: st3(h, dst, so),
                  reads=[so], writes=[io["s1"]], dma=True, tag=f"{tag}5st{i}")
        p.full_barrier(f"{tag}_end5")
    ctx.close()


def load_weight_bf16r(p, ctx, staging, arena, arena_off, dram2d, in_dim, out_dim, tag):
    nc = p.nc
    SL = staging[0].shape[1]
    i = 0
    nchunk = (in_dim + 127) // 128
    for c in range(nchunk):
        rows = min(128, in_dim - c * 128)
        for f0 in range(0, out_dim, SL):
            fl = min(SL, out_dim - f0)
            st = staging[i % 2]
            p.add("sync", lambda h, st=st, c=c, f0=f0, fl=fl, rows=rows, dram2d=dram2d:
                  h.dma_start(out=st[0:rows, 0:fl], in_=dram2d[c * 128:c * 128 + rows, f0:f0 + fl]),
                  reads=[dram2d], writes=[st], dma=True, tag=f"{tag}wld{i}")
            off = arena_off + c * out_dim + f0
            p.add("vector", lambda h, st=st, off=off, fl=fl, rows=rows, arena=arena:
                  h.tensor_copy(out=arena[0:rows, off:off + fl], in_=st[0:rows, 0:fl]),
                  reads=[st], writes=[arena], tag=f"{tag}wcast{i}")
            i += 1


class L1Shared:

    def __init__(self, p, ctx, io, tag="lf"):
        nc = p.nc
        self.lvecs = sb(nc, ctx, f"{tag}_lv", [128, 6 * 18])
        self.b1v = sb(nc, ctx, f"{tag}_b1", [128, 6 * 44])
        self.b2v = sb(nc, ctx, f"{tag}_b2", [128, 6 * 8])
        self.rqt = sb(nc, ctx, f"{tag}_rqt", [128, 128], BF16)
        self.blk64 = sb(nc, ctx, f"{tag}_blk", [128, 128])
        self.blk64b = sb(nc, ctx, f"{tag}_blkb", [128, 128], BF16)
        self.ones = sb(nc, ctx, f"{tag}_ones", [128, 128])
        self.m1 = sb(nc, ctx, f"{tag}_m1", [128, 128])
        self.m2 = sb(nc, ctx, f"{tag}_m2", [128, 128])
        self.identbf = sb(nc, ctx, f"{tag}_idb", [128, 128], BF16)
        self.tabb = {nm: sb(nc, ctx, f"{tag}_{nm}b", [128, T], BF16) for nm in ("CS8", "SS8", "CI", "SI")}
        for nm, t_ in [("lvecs", self.lvecs), ("b1P", self.b1v), ("b2", self.b2v),
                       ("rqt", self.rqt), ("blk64", self.blk64), ("ones", self.ones),
                       ("m1", self.m1), ("m2", self.m2), ("identbf", self.identbf)]:
            p.add("sync", lambda h, t_=t_, nm=nm, io=io: h.dma_start(out=t_[:], in_=io[nm]),
                  reads=[io[nm]], writes=[t_], dma=True, tag=f"{tag}ld{nm}")
        p.add("vector", lambda h, self=self: h.tensor_copy(out=self.blk64b[:], in_=self.blk64[:]),
              reads=[self.blk64], writes=[self.blk64b], tag=f"{tag}blkc")


def emit_l1_init_z(p, ctx, io, tag="lf"):
    nc = p.nc
    from contextlib import ExitStack
    TT = 512
    with ExitStack() as sctx:
        zt = sb(nc, sctx, f"{tag}zi_t", [128, NCH * TT])
        for i in range(NSEQ * (T // TT)):
            s, j = divmod(i, T // TT)
            p.add("sync", lambda h, src=fm_slice(io["xn1"], s, j, TT), zt=zt: ld3(h, zt, src),
                  reads=[io["xn1"]], writes=[zt], dma=True, tag=f"{tag}zc{i}")
            p.add("gpsimd", lambda h, dst=fm_slice(io["zbuf"], s, j, TT), zt=zt: st3(h, dst, zt),
                  reads=[zt], writes=[io["zbuf"]], dma=True, tag=f"{tag}zs{i}")
        p.full_barrier(f"{tag}_endinit")


def emit_l1_layer_attn(p, ctx, sh, io, cc, li, tag=None):
    nc = p.nc
    from contextlib import ExitStack
    tag = tag or f"lfa{li}"
    TT = 512
    NTT = T // TT
    zin = io["zbuf"]
    zout = io["zbuf2"]
    qcol = sh.lvecs[:, li * 18 + 16:li * 18 + 17]
    kcol = sh.lvecs[:, li * 18 + 17:li * 18 + 18]
    with ExitStack() as sctx:
        staging = [sb(nc, sctx, f"{tag}_stg{b}", [128, 256]) for b in range(2)]
        arena = sb(nc, sctx, f"{tag}_ar", [128, 8 * 3072 + 8 * 1024], BF16)
        for nm_ in ("CS8", "SS8", "CI", "SI"):
            for qtr in range(T // 256):
                st_ = staging[qtr % 2]
                p.add("sync", lambda h, nm_=nm_, qtr=qtr, st_=st_, io=io: h.dma_start(
                    out=st_[:], in_=io[nm_][:, qtr * 256:(qtr + 1) * 256]),
                    reads=[io[nm_]], writes=[st_], dma=True, tag=f"{tag}tl{nm_}.{qtr}")
                p.add("vector", lambda h, nm_=nm_, qtr=qtr, st_=st_, sh=sh: h.tensor_copy(
                    out=sh.tabb[nm_][:, qtr * 256:(qtr + 1) * 256], in_=st_[:]),
                    reads=[st_], writes=[sh.tabb[nm_]], tag=f"{tag}tb{nm_}.{qtr}")
        load_weight_bf16(p, sctx, staging, arena, 0, io["Wqkv"][li], 1024, 3072, f"{tag}qkv")
        load_weight_bf16(p, sctx, staging, arena, 8 * 3072, io["Wo"][li], 1024, 1024, f"{tag}wo")
        wqkv = arena[:, 0:8 * 3072].rearrange("p (c f) -> p c f", f=3072)
        wo = arena[:, 8 * 3072:].rearrange("p (c f) -> p c f", f=1024)
        zt = [sb(nc, sctx, f"{tag}_z{b}", [128, NCH * (TT + 1)]) for b in range(2)]
        zshb = sb(nc, sctx, f"{tag}_zs", [128, NCH * TT], BF16)
        q_sb = [sb(nc, sctx, f"{tag}_q{b}", [128, NCH * TT], BF16) for b in range(2)]
        k_sb = sb(nc, sctx, f"{tag}_k", [128, 8 * 1024], BF16)
        v_sb = sb(nc, sctx, f"{tag}_v", [128, 8 * 1024], BF16)
        att = sb(nc, sctx, f"{tag}_at", [128, NCH * TT], BF16)
        qf = [sb(nc, sctx, f"{tag}_qf{b}", [128, TT]) for b in range(2)]
        qsq = [sb(nc, sctx, f"{tag}_qsq{b}", [128, TT], BF16) for b in range(2)]
        qn = [sb(nc, sctx, f"{tag}_qn{b}", [128, TT], BF16) for b in range(2)]
        rotb = [sb(nc, sctx, f"{tag}_rb", [128, TT], BF16)] * 2
        t1 = [sb(nc, sctx, f"{tag}_t1", [128, TT], BF16)] * 2
        t2 = [sb(nc, sctx, f"{tag}_t2", [128, TT], BF16)] * 2
        negm = [sb(nc, sctx, f"{tag}_nm{b}", [128, 4]) for b in range(3)]
        den = [sb(nc, sctx, f"{tag}_dn{b}", [128, 4]) for b in range(3)]
        psb = [sb(nc, sctx, f"{tag}_p{b}", [128, 4 * 384], BF16) for b in range(3)]
        ptb = [sb(nc, sctx, f"{tag}_pt", [128, 4 * 384], BF16)] * 2
        ps_a = [ps(nc, sctx, f"{tag}_pa{b}", [128, TT]) for b in range(2)]
        ps_s = ps(nc, sctx, f"{tag}_ps", [128, 4 * 512])
        ps_t = ps(nc, sctx, f"{tag}_ptr", [128, 384], BF16)
        ps_o = ps(nc, sctx, f"{tag}_po", [128, 256])
        kv3 = k_sb[:].rearrange("p (c t) -> p c t", t=1024)
        vv3 = v_sb[:].rearrange("p (b f) -> p b f", f=1024)
        av_ = att[:].rearrange("p (c t) -> p c t", t=TT)

        def u_pre(i):
            s, j = divmod(i, NTT)
            jp = j % 2
            t0 = j * TT
            lo = max(t0 - 1, 0)
            ztb = zt[jp]
            src = zin[s].rearrange("(c p) t -> p c t", p=128)[:, :, lo:t0 + TT]
            def go(j=j, jp=jp, ztb=ztb, src=src, i=i):
                if j == 0:
                    p.add("vector", lambda h, ztb=ztb: h.memset(ztb[:], 0.0),
                          reads=[], writes=[ztb], tag=f"{tag}zz{i}")
                p.add("sync", lambda h, src=src, ztb=ztb, j=j, TT=TT: h.dma_start(
                    out=ztb[:].rearrange("p (c t) -> p c t", t=TT + 1)[:, :, (1 if j == 0 else 0):],
                    in_=src),
                    reads=[zin], writes=[ztb], dma=True, tag=f"{tag}zl{i}")
                ztv = ztb[:].rearrange("p (c t) -> p c t", t=TT + 1)
                zsv = zshb[:].rearrange("p (c t) -> p c t", t=TT)
                for c in range(NCH):
                    c0 = 1 if c < 4 else 0
                    p.add("vector", lambda h, c=c, c0=c0, ztv=ztv, zsv=zsv, ztb=ztb, zshb=zshb, TT=TT:
                          h.tensor_copy(out=zsv[:, c, :], in_=ztv[:, c, c0:c0 + TT]),
                          reads=[ztb], writes=[zshb], tag=f"{tag}sh{i}.{c}")
            return go

        def part1(i, fc):
            s, j = divmod(i, NTT)
            jp = j % 2
            b = fc % 2
            pa, pn = ps_a[b], ps_a[1 - b]
            fp = fc % 2
            zsv = zshb[:].rearrange("p (c t) -> p c t", t=TT)
            for dc in range(NCH):
                p.add("tensor", lambda h, fc=fc, dc=dc, pa=pa, zsv=zsv, wqkv=wqkv, zshb=zshb, arena=arena:
                      h.matmul(out=pa[:], lhsT=wqkv[:, dc, fc * 128:(fc + 1) * 128],
                               rhs=zsv[:, dc, :], start=(dc == 0), stop=(dc == NCH - 1)),
                      reads=[zshb, arena], writes=[pa], tag=f"{tag}qk{i}.{fc}.{dc}")
            qsqc, qfc = qsq[fp], qf[fp]
            p.add("scalar", lambda h, pa=pa, qsqc=qsqc: h.activation(out=qsqc[:], in_=pa[:], func=AF.Square),
                  reads=[pa], writes=[qsqc], tag=f"{tag}sq{i}.{fc}")
            p.add("tensor", lambda h, pn=pn, qsqc=qsqc, sh=sh: h.matmul(
                out=pn[:], lhsT=sh.blk64b[:], rhs=qsqc[:], start=True, stop=True),
                reads=[sh.blk64b, qsqc], writes=[pn], tag=f"{tag}nm{i}.{fc}")
            p.add("scalar", lambda h, pn=pn, qfc=qfc, cc=cc: h.activation(
                out=qfc[:], in_=pn[:], func=AF.Ln, bias=cc.col(1e-26)),
                reads=[pn, cc.tile], writes=[qfc], tag=f"{tag}sd{i}.{fc}")
            p.add("scalar", lambda h, qfc=qfc, qsqc=qsqc, cc=cc: h.activation(
                out=qsqc[:], in_=qfc[:], func=AF.Exp, scale=cc.col(-0.5)),
                reads=[qfc, cc.tile], writes=[qsqc], tag=f"{tag}rc{i}.{fc}")
            scol = qcol if fc < 8 else kcol
            p.add("vector", lambda h, qsqc=qsqc, qfc=qfc, scol=scol, sh=sh: h.tensor_scalar(
                out=qfc[:], in0=qsqc[:], scalar1=1e12, scalar2=scol,
                op0=ALU.min, op1=ALU.mult),
                reads=[qsqc, sh.lvecs], writes=[qfc], tag=f"{tag}rs{i}.{fc}")
            p.add("vector", lambda h, pa=pa, qfc=qfc, qnc=qn[fp]: h.tensor_mul(out=qnc[:], in0=pa[:], in1=qfc[:]),
                  reads=[pa, qfc], writes=[qn[fp]], tag=f"{tag}qn{i}.{fc}")

        def part2(i, fc):
            s, j = divmod(i, NTT)
            jp = j % 2
            t0 = j * TT
            fp = fc % 2
            isq = fc < 8
            prot = ps_a[fc % 2]
            qnc, rbc, t1c, t2c = qn[fp], rotb[fp], t1[fp], t2[fp]
            p.add("tensor", lambda h, prot=prot, qnc=qnc, sh=sh: h.matmul(
                out=prot[:], lhsT=sh.rqt[:], rhs=qnc[:], start=True, stop=True),
                reads=[sh.rqt, qnc], writes=[prot], tag=f"{tag}rt{i}.{fc}")
            p.add("scalar", lambda h, prot=prot, rbc=rbc: h.activation(out=rbc[:], in_=prot[:], func=AF.Copy),
                  reads=[prot], writes=[rbc], tag=f"{tag}rte{i}.{fc}")
            ctab = sh.tabb["CS8" if isq else "CI"]
            stab = sh.tabb["SS8" if isq else "SI"]
            p.add("vector", lambda h, qnc=qnc, t1c=t1c, ctab=ctab, t0=t0, TT=TT: h.tensor_mul(
                out=t1c[:], in0=qnc[:], in1=ctab[:, t0:t0 + TT]),
                reads=[qnc, ctab], writes=[t1c], tag=f"{tag}t1{i}.{fc}")
            p.add("vector", lambda h, rbc=rbc, t2c=t2c, stab=stab, t0=t0, TT=TT: h.tensor_mul(
                out=t2c[:], in0=rbc[:], in1=stab[:, t0:t0 + TT]),
                reads=[rbc, stab], writes=[t2c], tag=f"{tag}t2{i}.{fc}")
            if isq:
                dv = q_sb[jp][:].rearrange("p (c t) -> p c t", t=TT)[:, fc, :]
                p.add("vector", lambda h, t1c=t1c, t2c=t2c, dv=dv, dstt=q_sb[jp]: h.tensor_add(
                    out=dv, in0=t1c[:], in1=t2c[:]),
                    reads=[t1c, t2c], writes=[q_sb[jp]], tag=f"{tag}qs{i}.{fc}")
            else:
                ph = fc - 8
                rw0 = (j % 2) * 512
                for q4 in range(4):
                    dv = kv3[:, ph, rw0 + q4 * 128:rw0 + (q4 + 1) * 128]
                    p.add("vector", lambda h, t1c=t1c, t2c=t2c, dv=dv, q4=q4: h.tensor_add(
                        out=dv, in0=t1c[:, q4 * 128:(q4 + 1) * 128],
                        in1=t2c[:, q4 * 128:(q4 + 1) * 128]),
                        reads=[t1c, t2c], writes=[f"k{(j * 4 + q4) % 8}"],
                        tag=f"{tag}qs{i}.{fc}.{q4}")

        def u_fc(i, fc):
            def go(i=i, fc=fc):
                part1(i, fc)
                if fc > 0:
                    part2(i, fc - 1)
            return go

        def u_tail(i):
            def go(i=i):
                part2(i, 15)
            return go

        def u_v(i, ts4):
            s, j = divmod(i, NTT)
            blk = (j * 4 + ts4) % 8
            zsv = zshb[:].rearrange("p (c t) -> p c t", t=TT)
            def go(i=i, ts4=ts4, blk=blk, zsv=zsv):
                for half in range(2):
                    pa = ps_a[half]
                    for dc in range(NCH):
                        p.add("tensor", lambda h, ts4=ts4, half=half, dc=dc, pa=pa, zsv=zsv, wqkv=wqkv, zshb=zshb, arena=arena:
                              h.matmul(out=pa[:], lhsT=zsv[:, dc, ts4 * 128:(ts4 + 1) * 128],
                                       rhs=wqkv[:, dc, 2048 + half * 512:2048 + (half + 1) * 512],
                                       start=(dc == 0), stop=(dc == NCH - 1)),
                              reads=[zshb, arena], writes=[pa], tag=f"{tag}v{i}.{ts4}.{half}.{dc}")
                    p.add("scalar", lambda h, pa=pa, blk=blk, half=half, vv3=vv3, v_sb=v_sb: h.activation(
                        out=vv3[:, blk, half * 512:(half + 1) * 512],
                        in_=pa[:], func=AF.Copy),
                        reads=[pa], writes=[f"v{blk}"], tag=f"{tag}ve{i}.{ts4}.{half}")
            return go

        def A_units(i):
            return [u_pre(i)] + [u_fc(i, fc) for fc in range(16)] + [u_tail(i)] + \
                   [u_v(i, ts4) for ts4 in range(4)]

        def cg_env(i, c, grp):
            s, j = divmod(i, NTT)
            jp = j % 2
            qcg = j * 4 + c
            avail = min(qcg, 2)
            a0 = (2 - avail) * 128
            blocks = list(range(qcg - avail, qcg + 1))
            kkeys = [f"k{b_ % 8}" for b_ in blocks]
            runs = []
            rs_ = 0
            for m in range(1, len(blocks) + 1):
                if m == len(blocks) or (blocks[m] % 8) != (blocks[m - 1] % 8) + 1:
                    runs.append(((blocks[rs_] % 8) * 128, a0 + rs_ * 128, (m - rs_) * 128))
                    rs_ = m
            return jp, qcg, avail, a0, runs, kkeys

        def u_cg_a(i, c, grp):
            jp, qcg, avail, a0, runs, kkeys = cg_env(i, c, grp)
            def go(i=i, c=c, grp=grp, jp=jp, qcg=qcg, avail=avail, a0=a0, runs=runs, kkeys=kkeys):
                gp = (c * 4 + grp) % 3
                psbg, negmg, deng = psb[gp], negm[gp], den[gp]
                for hh4 in range(4):
                    hglob = grp * 4 + hh4
                    qsl = q_sb[jp][:].rearrange("p (c t) -> p c t", t=TT)[
                        (hglob % 2) * 64:(hglob % 2) * 64 + 64, hglob // 2, c * 128:(c + 1) * 128]
                    for (rc0, lc0, w_) in runs:
                        ksl = kv3[(hglob % 2) * 64:(hglob % 2) * 64 + 64, hglob // 2, rc0:rc0 + w_]
                        p.add("tensor", lambda h, hh4=hh4, qsl=qsl, ksl=ksl, lc0=lc0, w_=w_, ps_s=ps_s:
                              h.matmul(out=ps_s[:, hh4 * 512 + lc0:hh4 * 512 + lc0 + w_],
                                       lhsT=qsl, rhs=ksl, start=True, stop=True),
                              reads=[q_sb[jp]] + kkeys, writes=[ps_s],
                              tag=f"{tag}S{i}.{c}.{grp}.{hh4}")
                if avail == 2:
                    p.add("vector", lambda h, ps_s=ps_s, sh=sh: h.scalar_tensor_tensor(
                        out=ps_s[:].rearrange("p (g t) -> p g t", t=512)[:, :, 0:128],
                        in0=ps_s[:].rearrange("p (g t) -> p g t", t=512)[:, :, 0:128],
                        scalar=0.0, in1=sh.m1[:].unsqueeze(1).broadcast_to([128, 4, 128]),
                        op0=ALU.add, op1=ALU.add),
                        reads=[ps_s, sh.m1], writes=[ps_s], tag=f"{tag}M1{i}.{c}.{grp}")
                p.add("vector", lambda h, ps_s=ps_s, sh=sh: h.scalar_tensor_tensor(
                    out=ps_s[:].rearrange("p (g t) -> p g t", t=512)[:, :, 256:384],
                    in0=ps_s[:].rearrange("p (g t) -> p g t", t=512)[:, :, 256:384],
                    scalar=0.0, in1=sh.m2[:].unsqueeze(1).broadcast_to([128, 4, 128]),
                    op0=ALU.add, op1=ALU.add),
                    reads=[ps_s, sh.m2], writes=[ps_s], tag=f"{tag}M2{i}.{c}.{grp}")
                p.add("vector", lambda h, ps_s=ps_s, negm=negmg, a0=a0: h.reduce_max(
                    out=negm[:], in_=ps_s[:].rearrange("p (g t) -> p g t", t=512)[:, :, a0:384],
                    axis=AX.X),
                    reads=[ps_s], writes=[negmg], tag=f"{tag}mx{i}.{c}.{grp}")
                p.add("vector", lambda h, negm=negmg: h.tensor_scalar_mul(
                    out=negm[:], in0=negm[:], scalar1=-1.0),
                    reads=[negmg], writes=[negmg], tag=f"{tag}ng{i}.{c}.{grp}")
                for hh4 in range(4):
                    p.add("scalar", lambda h, hh4=hh4, ps_s=ps_s, psb=psbg, negm=negmg, den=deng, a0=a0:
                          h.activation(out=psb[:, hh4 * 384 + a0:hh4 * 384 + 384],
                                       in_=ps_s[:, hh4 * 512 + a0:hh4 * 512 + 384],
                                       func=AF.Exp, bias=negm[:, hh4:hh4 + 1],
                                       accum_out=den[:, hh4:hh4 + 1]),
                          reads=[ps_s, negmg], writes=[psbg, deng], tag=f"{tag}ex{i}.{c}.{grp}.{hh4}")
                p.add("vector", lambda h, den=deng: h.reciprocal(out=den[:], in_=den[:]),
                      reads=[deng], writes=[deng], tag=f"{tag}rc{i}.{c}.{grp}")
                for hh4 in range(4):
                    p.add("vector", lambda h, hh4=hh4, psb=psbg, den=deng, a0=a0: h.tensor_scalar_mul(
                        out=psb[:, hh4 * 384 + a0:hh4 * 384 + 384],
                        in0=psb[:, hh4 * 384 + a0:hh4 * 384 + 384], scalar1=den[:, hh4:hh4 + 1]),
                        reads=[psbg, deng], writes=[psbg], tag=f"{tag}nr{i}.{c}.{grp}.{hh4}")
            return go

        def u_cg_b(i, c, grp):
            jp, qcg, avail, a0, runs, kkeys = cg_env(i, c, grp)
            def go(i=i, c=c, grp=grp, jp=jp, qcg=qcg, avail=avail, a0=a0):
                gp = (c * 4 + grp) % 3
                psbg, ptbg = psb[gp], ptb[0]
                for hh4 in range(4):
                    nblk = avail + 1
                    for bi in range(nblk):
                        pcol = a0 + bi * 128
                        p.add("tensor", lambda h, hh4=hh4, bi=bi, pcol=pcol, psb=psbg, ps_t=ps_t, sh=sh:
                              h.transpose(out=ps_t[:, pcol:pcol + 128],
                                          in_=psb[:, hh4 * 384 + pcol:hh4 * 384 + pcol + 128],
                                          identity=sh.identbf[:]),
                              reads=[psbg, sh.identbf], writes=[ps_t], tag=f"{tag}tr{i}.{c}.{grp}.{hh4}.{bi}")
                    p.add("vector", lambda h, hh4=hh4, a0=a0, ps_t=ps_t, ptb=ptbg: h.tensor_copy(
                        out=ptb[:, hh4 * 384 + a0:hh4 * 384 + 384], in_=ps_t[:, a0:384]),
                        reads=[ps_t], writes=[ptbg], tag=f"{tag}te{i}.{c}.{grp}.{hh4}")
                for hh4 in range(4):
                    hglob = grp * 4 + hh4
                    nblk = avail + 1
                    orow = (hglob % 2) * 64
                    ocol = (hh4 // 2) * 128
                    for bi in range(nblk):
                        kblk = (qcg - avail + bi) % 8
                        pcol = a0 + bi * 128
                        vsl = vv3[:, kblk, hglob * 64:hglob * 64 + 64]
                        p.add("tensor", lambda h, vsl=vsl, hh4=hh4, bi=bi, pcol=pcol, orow=orow, ocol=ocol, ptb=ptbg, ps_o=ps_o, nblk=nblk:
                              h.matmul(out=ps_o[orow:orow + 64, ocol:ocol + 128],
                                       lhsT=vsl, rhs=ptb[:, hh4 * 384 + pcol:hh4 * 384 + pcol + 128],
                                       start=(hh4 <= 1 and bi == 0), stop=(hh4 == 3 and bi == nblk - 1),
                                       skip_group_check=True),
                              reads=[f"v{kblk}", ptbg], writes=[ps_o], tag=f"{tag}O{i}.{c}.{grp}.{hh4}.{bi}")
                p.add("scalar", lambda h, grp=grp, c=c, ps_o=ps_o, av_=av_, att=att: h.activation(
                    out=av_[:, grp * 2:grp * 2 + 2, c * 128:(c + 1) * 128],
                    in_=ps_o[:].rearrange("p (a b) -> p a b", b=128), func=AF.Copy),
                    reads=[ps_o], writes=[att], tag=f"{tag}oe{i}.{c}.{grp}")
            return go

        def u_wo(i):
            s, j = divmod(i, NTT)
            jp = j % 2
            def go(i=i, s=s, j=j, jp=jp):
                ztv = zt[jp][:].rearrange("p (c t) -> p c t", t=TT + 1)
                for fc in range(NCH):
                    pa = ps_a[fc % 2]
                    for dc in range(NCH):
                        p.add("tensor", lambda h, fc=fc, dc=dc, pa=pa, av_=av_, wo=wo, att=att, arena=arena:
                              h.matmul(out=pa[:], lhsT=wo[:, dc, fc * 128:(fc + 1) * 128],
                                       rhs=av_[:, dc, :], start=(dc == 0), stop=(dc == NCH - 1)),
                              reads=[att, arena], writes=[pa], tag=f"{tag}wo{i}.{fc}.{dc}")
                    p.add("vector", lambda h, pa=pa, fc=fc, ztv=ztv, ztb=zt[jp], TT=TT: h.tensor_add(
                        out=ztv[:, fc, 1:TT + 1], in0=pa[:], in1=ztv[:, fc, 1:TT + 1]),
                        reads=[pa, zt[jp]], writes=[zt[jp]], tag=f"{tag}wr{i}.{fc}")
                p.add("gpsimd", lambda h, dst=fm_slice(zout, s, j, TT), ztv=ztv, ztb=zt[jp], TT=TT: h.dma_start(
                    out=dst, in_=ztv[:, :, 1:TT + 1]),
                    reads=[zt[jp]], writes=[zout], dma=True, tag=f"{tag}zst{i}")
            return go

        def B_units(i):
            cgs = [(c, grp) for c in range(4) for grp in range(4)]
            units = [u_cg_a(i, *cgs[0]), u_cg_a(i, *cgs[1])]
            for m in range(2, len(cgs)):
                units.append(u_cg_a(i, *cgs[m]))
                units.append(u_cg_b(i, *cgs[m - 2]))
            units.append(u_cg_b(i, *cgs[-2]))
            units.append(u_cg_b(i, *cgs[-1]))
            units.append(u_wo(i))
            return units

        ntile = NSEQ * NTT
        for u in A_units(0):
            u()
        for i in range(ntile):
            bu = B_units(i)
            au = A_units(i + 1) if i + 1 < ntile else []
            ai = 0
            for x_, u in enumerate(bu):
                u()
                tgt = (x_ + 1) * len(au) // len(bu)
                while ai < tgt:
                    au[ai]()
                    ai += 1
            while ai < len(au):
                au[ai]()
                ai += 1
        p.full_barrier(f"{tag}_end")


def emit_l1_layer_ffn(p, ctx, sh, io, cc, li, tag=None):
    nc = p.nc
    from contextlib import ExitStack
    tag = tag or f"lff{li}"
    TT = 512
    NTT = T // TT
    zd = io["zbuf2"]
    zdo = io["zbuf"]
    NPAIR = 22
    rows_of = [128] * 21 + [42]
    with ExitStack() as sctx:
        staging = [sb(nc, sctx, f"{tag}_stg{b}", [128, 1024]) for b in range(2)]
        arena1 = sb(nc, sctx, f"{tag}_ar1", [128, 8 * 5460], BF16)
        load_weight_bf16(p, sctx, staging, arena1, 0, io["W1P"][li], 1024, 5460, f"{tag}w1")
        w1v = arena1[:].rearrange("p (c f) -> p c f", f=5460)
        zsh = sb(nc, sctx, f"{tag}_zs", [128, NCH * TT])
        x2b = sb(nc, sctx, f"{tag}_x2", [128, NCH * TT], BF16)
        hsb = sb(nc, sctx, f"{tag}_h", [128, NPAIR * TT], BF16)
        af = [sb(nc, sctx, f"{tag}_af{b}", [128, TT]) for b in range(2)]
        gg = [sb(nc, sctx, f"{tag}_gg{b}", [128, TT]) for b in range(2)]
        ps_f = [ps(nc, sctx, f"{tag}_pf{b}", [128, TT]) for b in range(4)]
        ps_l = [ps(nc, sctx, f"{tag}_pl{b}", [128, TT]) for b in range(2)]
        ln = LNScratch(nc, sctx, f"{tag}_ln", TT, ps_l)
        for s in range(NSEQ):
            for j in range(NTT):
                i = s * NTT + j
                t0 = j * TT
                zsv = zsh[:].rearrange("p (c t) -> p c t", t=TT)
                zview = zd[s].rearrange("(c p) t -> p c t", p=128)
                p.add("sync", lambda h, zsv=zsv, zview=zview, t0=t0, TT=TT, zsh=zsh: h.dma_start(
                    out=zsv[:, 0:4, :], in_=zview[:, 0:4, t0:t0 + TT]),
                    reads=[zd], writes=[zsh], dma=True, tag=f"{tag}zl{i}a")
                if j == 0:
                    p.add("vector", lambda h, zsv=zsv, zsh=zsh: h.memset(zsv[:, 4:8, 0:1], 0.0),
                          reads=[], writes=[zsh], tag=f"{tag}zz{i}")
                    p.add("sync", lambda h, zsv=zsv, zview=zview, TT=TT, zsh=zsh: h.dma_start(
                        out=zsv[:, 4:8, 1:TT], in_=zview[:, 4:8, 0:TT - 1]),
                        reads=[zd], writes=[zsh], dma=True, tag=f"{tag}zl{i}b")
                else:
                    p.add("sync", lambda h, zsv=zsv, zview=zview, t0=t0, TT=TT, zsh=zsh: h.dma_start(
                        out=zsv[:, 4:8, :], in_=zview[:, 4:8, t0 - 1:t0 + TT - 1]),
                        reads=[zd], writes=[zsh], dma=True, tag=f"{tag}zl{i}b")
                emit_ln_fm_tile(p, ln, zsh, x2b, sh.ones, cc, f"{tag}ln{i}",
                                gamma=(sh.lvecs, li * 18), beta=(sh.lvecs, li * 18 + 8))
                x2v = x2b[:].rearrange("p (c t) -> p c t", t=TT)
                hv = hsb[:].rearrange("p (c t) -> p c t", t=TT)
                for pc in range(NPAIR):
                    rows = rows_of[pc]
                    if pc < 21:
                        acol, gcol = 2 * pc * 128, (2 * pc + 1) * 128
                    else:
                        acol, gcol = 5376, 5418
                    pfa, pfg = ps_f[(pc % 2) * 2], ps_f[(pc % 2) * 2 + 1]
                    for dc in range(NCH):
                        p.add("tensor", lambda h, dc=dc, acol=acol, rows=rows, pfa=pfa, w1v=w1v, x2v=x2v, x2b=x2b, arena1=arena1:
                              h.matmul(out=pfa[0:rows, :], lhsT=w1v[:, dc, acol:acol + rows],
                                       rhs=x2v[:, dc, :], start=(dc == 0), stop=(dc == NCH - 1)),
                              reads=[x2b, arena1], writes=[pfa], tag=f"{tag}a{i}.{pc}.{dc}")
                    for dc in range(NCH):
                        p.add("tensor", lambda h, dc=dc, gcol=gcol, rows=rows, pfg=pfg, w1v=w1v, x2v=x2v, x2b=x2b, arena1=arena1:
                              h.matmul(out=pfg[0:rows, :], lhsT=w1v[:, dc, gcol:gcol + rows],
                                       rhs=x2v[:, dc, :], start=(dc == 0), stop=(dc == NCH - 1)),
                              reads=[x2b, arena1], writes=[pfg], tag=f"{tag}g{i}.{pc}.{dc}")
                    afc, ggc = af[pc % 2], gg[pc % 2]
                    p.add("scalar", lambda h, pc=pc, rows=rows, pfg=pfg, ggc=ggc, sh=sh, li=li:
                          h.activation(out=ggc[0:rows, :], in_=pfg[0:rows, :], func=AF.Gelu,
                                       bias=sh.b1v[0:rows, li * 44 + 2 * pc + 1:li * 44 + 2 * pc + 2]),
                          reads=[pfg, sh.b1v], writes=[ggc], tag=f"{tag}gel{i}.{pc}")
                    p.add("vector", lambda h, pc=pc, rows=rows, pfa=pfa, afc=afc, sh=sh, li=li:
                          h.tensor_scalar_add(out=afc[0:rows, :], in0=pfa[0:rows, :],
                                              scalar1=sh.b1v[0:rows, li * 44 + 2 * pc:li * 44 + 2 * pc + 1]),
                          reads=[pfa, sh.b1v], writes=[afc], tag=f"{tag}ab{i}.{pc}")
                    p.add("vector", lambda h, pc=pc, rows=rows, afc=afc, ggc=ggc, hv=hv, hsb=hsb:
                          h.tensor_mul(out=hv[0:rows, pc, :], in0=afc[0:rows, :], in1=ggc[0:rows, :]),
                          reads=[afc, ggc], writes=[hsb], tag=f"{tag}h{i}.{pc}")
                if i == 0 and "dbgx2" in io:
                    p.add("gpsimd", lambda h, io=io, x2b=x2b: h.dma_start(out=io["dbgx2"], in_=x2b[:]),
                          reads=[x2b], writes=[io["dbgx2"]], dma=True, tag=f"{tag}dbgx2")
                    p.add("gpsimd", lambda h, io=io, hsb=hsb: h.dma_start(out=io["dbgh"], in_=hsb[:]),
                          reads=[hsb], writes=[io["dbgh"]], dma=True, tag=f"{tag}dbgh")
                hdst = io["hbuf"][s].rearrange("(c p) t -> p c t", p=128)[:, :, t0:t0 + TT]
                p.add("gpsimd", lambda h, hdst=hdst, hsb=hsb, TT=TT, NPAIR=NPAIR: h.dma_start(
                    out=hdst, in_=hsb[:].rearrange("p (c t) -> p c t", t=TT)),
                    reads=[hsb], writes=[io["hbuf"]], dma=True, tag=f"{tag}hst{i}")
        p.full_barrier(f"{tag}_endA")
    with ExitStack() as sctx:
        staging = [sb(nc, sctx, f"{tag}B_stg{b}", [128, 1024]) for b in range(2)]
        arena2 = sb(nc, sctx, f"{tag}B_ar2", [128, 22 * 1024], BF16)
        load_weight_bf16r(p, sctx, staging, arena2, 0, io["W2"][li], 2730, 1024, f"{tag}w2")
        w2v = arena2[:].rearrange("p (c f) -> p c f", f=1024)
        zt = [sb(nc, sctx, f"{tag}B_z{b}", [128, NCH * TT]) for b in range(2)]
        ht = [sb(nc, sctx, f"{tag}B_h{b}", [128, NPAIR * TT], BF16) for b in range(2)]
        ps_y = [ps(nc, sctx, f"{tag}B_py{b}", [128, TT]) for b in range(4)]
        for s in range(NSEQ):
            for j in range(NTT):
                i = s * NTT + j
                t0 = j * TT
                b = i % 2
                p.add("sync", lambda h, src=fm_slice(zd, s, j, TT), b=b, zt=zt: ld3(h, zt[b], src),
                      reads=[zd], writes=[zt[b]], dma=True, tag=f"{tag}Bzl{i}")
                hsrc = io["hbuf"][s].rearrange("(c p) t -> p c t", p=128)[:, :, t0:t0 + TT]
                p.add("sync", lambda h, hsrc=hsrc, b=b, ht=ht, TT=TT: h.dma_start(
                    out=ht[b][:].rearrange("p (c t) -> p c t", t=TT), in_=hsrc),
                    reads=[io["hbuf"]], writes=[ht[b]], dma=True, tag=f"{tag}Bhl{i}")
                hv = ht[b][:].rearrange("p (c t) -> p c t", t=TT)
                zv = zt[b][:].rearrange("p (c t) -> p c t", t=TT)
                for dc in range(NCH):
                    py = ps_y[dc % 4]
                    for pc in range(NPAIR):
                        rows = rows_of[pc]
                        p.add("tensor", lambda h, dc=dc, pc=pc, rows=rows, py=py, w2v=w2v, hv=hv, hsb2=ht[b], arena2=arena2, NPAIR=NPAIR:
                              h.matmul(out=py[:], lhsT=w2v[0:rows, pc, dc * 128:(dc + 1) * 128],
                                       rhs=hv[0:rows, pc, :], start=(pc == 0), stop=(pc == NPAIR - 1)),
                              reads=[ht[b], arena2], writes=[py], tag=f"{tag}y{i}.{dc}.{pc}")
                    p.add("vector", lambda h, py=py, dc=dc, zv=zv, zt2=zt[b], sh=sh, li=li:
                          h.scalar_tensor_tensor(out=zv[:, dc, :], in0=py[:],
                                                 scalar=sh.b2v[:, li * 8 + dc:li * 8 + dc + 1],
                                                 in1=zv[:, dc, :], op0=ALU.add, op1=ALU.add),
                          reads=[py, sh.b2v, zt[b]], writes=[zt[b]], tag=f"{tag}yr{i}.{dc}")
                p.add("gpsimd", lambda h, dst=fm_slice(zdo, s, j, TT), b=b, zt=zt: st3(h, dst, zt[b]),
                      reads=[zt[b]], writes=[zdo], dma=True, tag=f"{tag}Bzst{i}")
        p.full_barrier(f"{tag}_endB")


def emit_l1_final(p, ctx, io, cc, tag="l1fin"):
    nc = p.nc
    from contextlib import ExitStack
    TT = 512
    with ExitStack() as sctx:
        xt = [sb(nc, sctx, f"{tag}_x{b}", [128, NCH * TT]) for b in range(2)]
        st_ = [sb(nc, sctx, f"{tag}_s{b}", [128, NCH * TT]) for b in range(2)]
        ztl = [sb(nc, sctx, f"{tag}_zt{b}", [128, NCH * TT]) for b in range(2)]
        gt = sb(nc, sctx, f"{tag}_g", [128, NCH * TT])
        for i in range(NSEQ * (T // TT)):
            s, j = divmod(i, T // TT)
            b = i % 2
            p.add("sync", lambda h, src=fm_slice(io["x1"], s, j, TT), b=b, xt=xt: ld3(h, xt[b], src),
                  reads=[io["x1"]], writes=[xt[b]], dma=True, tag=f"{tag}lx{i}")
            p.add("sync", lambda h, src=fm_slice(io["s1"], s, j, TT), b=b, st_=st_: ld3(h, st_[b], src),
                  reads=[io["s1"]], writes=[st_[b]], dma=True, tag=f"{tag}ls{i}")
            p.add("sync", lambda h, src=fm_slice(io["zbuf"], s, j, TT), b=b, ztl=ztl: ld3(h, ztl[b], src),
                  reads=[io["zbuf"]], writes=[ztl[b]], dma=True, tag=f"{tag}lz{i}")
            p.add("scalar", lambda h, b=b, ztl=ztl, gt=gt: h.activation(out=gt[:], in_=ztl[b][:], func=AF.Silu),
                  reads=[ztl[b]], writes=[gt], tag=f"{tag}sl{i}")
            p.add("vector", lambda h, b=b, gt=gt, st_=st_: h.tensor_mul(out=gt[:], in0=gt[:], in1=st_[b][:]),
                  reads=[gt, st_[b]], writes=[gt], tag=f"{tag}m{i}")
            p.add("vector", lambda h, b=b, xt=xt, gt=gt: h.tensor_add(out=xt[b][:], in0=xt[b][:], in1=gt[:]),
                  reads=[xt[b], gt], writes=[xt[b]], tag=f"{tag}a{i}")
            p.add("gpsimd", lambda h, dst=fm_slice(io["out"], s, j, TT), b=b, xt=xt: st3(h, dst, xt[b]),
                  reads=[xt[b]], writes=[io["out"]], dma=True, tag=f"{tag}st{i}")
        p.full_barrier(f"{tag}_end")


import numpy as np
import ml_dtypes
from contextlib import ExitStack
from concourse.bass_utils import run_bass_kernel_spmd

NCORES = 8


def _pack_cols(v):
    n = v.size // 128
    return np.ascontiguousarray(v.reshape(n, 128).T)


def _moe_consts():
    i16 = np.tile(np.arange(16, dtype=np.float32), (128, 1))
    i4 = np.tile(np.arange(4, dtype=np.float32), (128, 1))
    ij = np.tile(np.tile(np.arange(4, dtype=np.float32), 4), (128, 1))
    sel = np.zeros((16, 2048), np.float32)
    for e in range(16):
        sel[e, e * 128:(e + 1) * 128] = 1.0
    return dict(
        c_ident=np.eye(128, dtype=np.float32), c_sel16=sel,
        c_iota16=i16, c_iota16mb=i16 - 65536.0,
        c_iotaj=ij, c_iotajmb=ij - 65536.0,
        c_iota4=i4, c_iota4mb=i4 - 65536.0)


def _l1_perm():
    perm = []
    for c in range(21):
        perm += list(range(c * 128, (c + 1) * 128))
        perm += list(range(2730 + c * 128, 2730 + (c + 1) * 128))
    perm += list(range(2688, 2730))
    perm += list(range(2730 + 2688, 5460))
    return np.array(perm)


def _rotary_tables(n=2048):
    DIM_HEAD, WINDOW = 64, 256
    half = np.arange(0, DIM_HEAD, 2, dtype=np.float32)
    inv_freq = (1.0 / 10000.0 ** (half / DIM_HEAD)).astype(np.float32)
    t = np.arange(n, dtype=np.float32)
    freqs = t[:, None] * inv_freq[None, :]
    freqs = np.concatenate([freqs, freqs], -1)
    base_scale = ((half + 0.4 * DIM_HEAD) / (1.4 * DIM_HEAD)).astype(np.float32)
    power = (t - n // 2) / (WINDOW // 2)
    scale = base_scale[None, :] ** power[:, None]
    scale = np.concatenate([scale, scale], -1).astype(np.float32)
    return freqs.astype(np.float32), scale


def _l1_consts():
    freqs, scale = _rotary_tables()
    cos, sin = np.cos(freqs), np.sin(freqs)
    dup = lambda a: np.ascontiguousarray(np.concatenate([a, a], 0), dtype=np.float32)
    RQ = np.zeros((128, 128), np.float32)
    for base in (0, 64):
        for r in range(32):
            RQ[base + r, base + r + 32] = -1.0
            RQ[base + 32 + r, base + r] = 1.0
    blk = np.zeros((128, 128), np.float32)
    blk[:64, :64] = 1.0
    blk[64:, 64:] = 1.0
    m1 = np.triu(np.full((128, 128), 0.0, np.float32))
    m1 = np.where(np.arange(128)[:, None] > np.arange(128)[None, :], -1e30, 0.0).astype(np.float32)
    m2 = np.where(np.arange(128)[:, None] < np.arange(128)[None, :], -1e30, 0.0).astype(np.float32)
    return dict(
        c_CS8=dup((cos * scale * 8.0).T), c_SS8=dup((sin * scale * 8.0).T),
        c_CI=dup((cos / scale).T), c_SI=dup((sin / scale).T),
        c_rqt=np.ascontiguousarray(RQ.T).astype(ml_dtypes.bfloat16),
        c_identbf=np.eye(128).astype(ml_dtypes.bfloat16),
        c_blk64=blk, c_ones128=np.ones((128, 128), np.float32),
        c_m1=m1, c_m2=m2)


def _build_launch_A():
    nc = bass.Bass("TRN2", target_bir_lowering=False, debug=False)
    F32d, BF16d = F32, BF16
    io = {}
    io["x"] = nc.dram_tensor("x", [NSEQ, D, T], F32d, kind="ExternalInput").ap()
    io["wqkvT"] = nc.dram_tensor("wqkvT", [3, 1024, 1024], F32d, kind="ExternalInput").ap()
    io["w1T"] = nc.dram_tensor("w1T", [1024, 4096], F32d, kind="ExternalInput").ap()
    io["w2T"] = nc.dram_tensor("w2T", [4096, 1024], F32d, kind="ExternalInput").ap()
    io["vecs"] = nc.dram_tensor("vecs", [128, 72], F32d, kind="ExternalInput").ap()
    io["ones"] = nc.dram_tensor("c_ones", [128, 128], F32d, kind="ExternalInput").ap()
    io["onesbf"] = nc.dram_tensor("c_onesbf", [128, 128], BF16d, kind="ExternalInput").ap()
    io["gate_wT"] = nc.dram_tensor("gate_wT", [1024, 16], F32d, kind="ExternalInput").ap()
    io["gate_bias"] = nc.dram_tensor("gate_bias", [1, 16], F32d, kind="ExternalInput").ap()
    io["w1Tall"] = nc.dram_tensor("w1Tall", [16, 1024, 768], F32d, kind="ExternalInput").ap()
    io["w3Tall"] = nc.dram_tensor("w3Tall", [16, 1024, 768], F32d, kind="ExternalInput").ap()
    io["w2Tall"] = nc.dram_tensor("w2Tall", [16, 768, 1024], F32d, kind="ExternalInput").ap()
    io["ws1T"] = nc.dram_tensor("ws1T", [1024, 768], F32d, kind="ExternalInput").ap()
    io["ws3T"] = nc.dram_tensor("ws3T", [1024, 768], F32d, kind="ExternalInput").ap()
    io["ws2T"] = nc.dram_tensor("ws2T", [768, 1024], F32d, kind="ExternalInput").ap()
    io["ident"] = nc.dram_tensor("c_ident", [128, 128], F32d, kind="ExternalInput").ap()
    io["sel16"] = nc.dram_tensor("c_sel16", [16, 2048], F32d, kind="ExternalInput").ap()
    for nm in ("iota16", "iota16mb", "iotaj", "iotajmb"):
        io[nm] = nc.dram_tensor("c_" + nm, [128, 16], F32d, kind="ExternalInput").ap()
    for nm in ("iota4", "iota4mb"):
        io[nm] = nc.dram_tensor("c_" + nm, [128, 4], F32d, kind="ExternalInput").ap()
    ccd = nc.dram_tensor("c_cols", [128, 32], F32d, kind="ExternalInput").ap()
    io["x1"] = nc.dram_tensor("x1", [NSEQ, D, T], F32d, kind="ExternalOutput").ap()
    io["xn1"] = nc.dram_tensor("xn1", [NSEQ, D, T], F32d, kind="ExternalOutput").ap()
    io["xn"] = nc.dram_tensor("xn", [NSEQ, D, T], F32d, kind="Internal").ap()
    io["s0"] = nc.dram_tensor("s0", [NSEQ, D, T], F32d, kind="Internal").ap()
    io["qT"] = nc.dram_tensor("qT", [NSEQ, D, T], BF16d, kind="Internal").ap()
    io["smid"] = nc.dram_tensor("smid", [NSEQ, D, T], F32d, kind="Internal").ap()
    io["hbuf"] = nc.dram_tensor("hbufA", [NSEQ, 4096, T], BF16d, kind="Internal").ap()
    io["xnbf"] = nc.dram_tensor("xnbf", [NSEQ, D, T], BF16d, kind="Internal").ap()
    io["ybuf"] = nc.dram_tensor("ybuf", [NSEQ, D, T], F32d, kind="Internal").ap()
    ctx = ExitStack()
    with ctx:
        p = Plan(nc, ctx)
        cc = ConstCols(nc, ctx)
        cc.load(p, ccd)
        ones_sb = sb(nc, ctx, "lnA_ones", [128, 128])
        p.add("sync", lambda h, ones_sb=ones_sb, io=io: h.dma_start(out=ones_sb[:], in_=io["ones"]),
              reads=[io["ones"]], writes=[ones_sb], dma=True, tag="ldones")
        emit_outer_ln(p, ctx, io["x"], io["xn"], ones_sb, cc, "ln0")
        p.full_barrier("after_ln0")
        emit_l0_attn(p, ctx, dict(io, xn=io["xn"], s0=io["s0"]), cc)
        emit_l0_moe(p, ctx, dict(io, x=io["x"], xn=io["xn"], s0=io["s0"], x1=io["x1"]), cc)
        emit_outer_ln(p, ctx, io["x1"], io["xn1"], ones_sb, cc, "ln1")
        p.add("gpsimd", lambda h, io=io: h.nop(), reads=[io["x1"], io["xn1"]], tag="final")
        p.emit()
    return nc, cc


def _build_launch_B():
    nc = bass.Bass("TRN2", target_bir_lowering=False, debug=False)
    io = {}
    io["xn1"] = nc.dram_tensor("xn1", [NSEQ, D, T], F32, kind="ExternalInput").ap()
    io["in_proj"] = nc.dram_tensor("in_proj", [1024, 4096], F32, kind="ExternalInput").ap()
    io["x_proj"] = nc.dram_tensor("x_proj", [2048, 2080], F32, kind="ExternalInput").ap()
    io["dt_proj"] = nc.dram_tensor("dt_proj", [2048, 2048], F32, kind="ExternalInput").ap()
    io["out_proj"] = nc.dram_tensor("out_proj", [2048, 1024], F32, kind="ExternalInput").ap()
    io["mvecs"] = nc.dram_tensor("mvecs", [128, 96], F32, kind="ExternalInput").ap()
    io["alog"] = nc.dram_tensor("alog", [128, 256], F32, kind="ExternalInput").ap()
    io["sel32"] = nc.dram_tensor("c_sel32", [32, 4096], F32, kind="ExternalInput").ap()
    ccd = nc.dram_tensor("c_cols", [128, 32], F32, kind="ExternalInput").ap()
    io["s1"] = nc.dram_tensor("s1", [NSEQ, D, T], F32, kind="ExternalOutput").ap()
    for nm, shp, dt in [("uraw", [NSEQ, 2048, T], F32), ("usil", [NSEQ, 2048, T], F32),
                        ("silg", [NSEQ, 2048, T], BF16), ("delt", [NSEQ, 2048, T], F32),
                        ("dpre", [NSEQ, 2048, T], BF16), ("bc", [NSEQ, 32, T], F32),
                        ("ymb", [NSEQ, 2048, T], BF16)]:
        io[nm] = nc.dram_tensor(nm, shp, dt, kind="Internal").ap()
    ctx = ExitStack()
    with ctx:
        p = Plan(nc, ctx)
        cc = ConstCols(nc, ctx)
        cc.load(p, ccd)
        emit_l1_mamba(p, ctx, io, cc)
        p.add("gpsimd", lambda h, io=io: h.nop(), reads=[io["s1"]], tag="final")
        p.emit()
    return nc, cc


def _build_launch_C():
    nc = bass.Bass("TRN2", target_bir_lowering=False, debug=False)
    io = {}
    for nm in ("xn1", "x1", "s1"):
        io[nm] = nc.dram_tensor(nm, [NSEQ, D, T], F32, kind="ExternalInput").ap()
    io["Wqkv"] = nc.dram_tensor("Wqkv", [6, 1024, 3072], F32, kind="ExternalInput").ap()
    io["Wo"] = nc.dram_tensor("Wo", [6, 1024, 1024], F32, kind="ExternalInput").ap()
    io["W1P"] = nc.dram_tensor("W1P", [6, 1024, 5460], F32, kind="ExternalInput").ap()
    io["W2"] = nc.dram_tensor("W2", [6, 2730, 1024], F32, kind="ExternalInput").ap()
    io["lvecs"] = nc.dram_tensor("lvecs", [128, 108], F32, kind="ExternalInput").ap()
    io["b1P"] = nc.dram_tensor("b1P", [128, 264], F32, kind="ExternalInput").ap()
    io["b2"] = nc.dram_tensor("b2v", [128, 48], F32, kind="ExternalInput").ap()
    for nm in ("CS8", "SS8", "CI", "SI"):
        io[nm] = nc.dram_tensor("c_" + nm, [128, T], F32, kind="ExternalInput").ap()
    io["rqt"] = nc.dram_tensor("c_rqt", [128, 128], BF16, kind="ExternalInput").ap()
    io["identbf"] = nc.dram_tensor("c_identbf", [128, 128], BF16, kind="ExternalInput").ap()
    io["ones"] = nc.dram_tensor("c_ones128", [128, 128], F32, kind="ExternalInput").ap()
    for nm in ("blk64", "m1", "m2"):
        io[nm] = nc.dram_tensor("c_" + nm, [128, 128], F32, kind="ExternalInput").ap()
    ccd = nc.dram_tensor("c_cols", [128, 32], F32, kind="ExternalInput").ap()
    io["out"] = nc.dram_tensor("out", [NSEQ, D, T], F32, kind="ExternalOutput").ap()
    io["zbuf"] = nc.dram_tensor("zbuf", [NSEQ, D, T], F32, kind="Internal").ap()
    io["zbuf2"] = nc.dram_tensor("zbuf2", [NSEQ, D, T], F32, kind="Internal").ap()
    io["hbuf"] = nc.dram_tensor("hbufC", [NSEQ, 2816, T], BF16, kind="Internal").ap()
    ctx = ExitStack()
    with ctx:
        p = Plan(nc, ctx)
        cc = ConstCols(nc, ctx)
        cc.load(p, ccd)
        sh = L1Shared(p, ctx, io)
        emit_l1_init_z(p, ctx, io)
        for li in range(6):
            emit_l1_layer_attn(p, ctx, sh, io, cc, li)
            emit_l1_layer_ffn(p, ctx, sh, io, cc, li)
        emit_l1_final(p, ctx, io, cc)
        p.add("gpsimd", lambda h, io=io: h.nop(), reads=[io["out"]], tag="final")
        p.emit()
    return nc, cc


_CACHE = {}


USE_FUSED = True


def kernel(**inputs):
    if USE_FUSED:
        return kernel_fused(**inputs)
    g = lambda n: np.asarray(inputs[n])
    x = g("L0_attn_x")
    x_fm = np.ascontiguousarray(x.transpose(0, 2, 1))
    shards = [np.ascontiguousarray(x_fm[c * NSEQ:(c + 1) * NSEQ]) for c in range(NCORES)]

    if "A" not in _CACHE:
        _CACHE["A"] = _build_launch_A()
        _CACHE["B"] = _build_launch_B()
        _CACHE["C"] = _build_launch_C()
    (ncA, ccA), (ncB, ccB), (ncC, ccC) = _CACHE["A"], _CACHE["B"], _CACHE["C"]

    vecs = np.zeros((128, 72), np.float32)
    vecs[:, 0:8] = _pack_cols(g("L0_attn_g1"))
    vecs[:, 8:16] = _pack_cols(g("L0_attn_be1"))
    vecs[:, 16:24] = _pack_cols(g("L0_attn_g2"))
    vecs[:, 24:32] = _pack_cols(g("L0_attn_be2"))
    vecs[:, 32:64] = _pack_cols(g("L0_attn_b1"))
    vecs[:, 64:72] = _pack_cols(g("L0_attn_b2"))
    mapA_common = dict(
        wqkvT=np.ascontiguousarray(np.stack([g("L0_attn_wq").T, g("L0_attn_wk").T, g("L0_attn_wv").T])),
        w1T=np.ascontiguousarray(g("L0_attn_w1").T),
        w2T=np.ascontiguousarray(g("L0_attn_w2").T),
        vecs=vecs,
        c_ones=np.ones((128, 128), np.float32),
        c_onesbf=np.ones((128, 128), ml_dtypes.bfloat16),
        gate_wT=np.ascontiguousarray(g("L0_ffn_gate_w").T),
        gate_bias=np.ascontiguousarray(g("L0_ffn_gate_bias").reshape(1, 16)),
        w1Tall=np.ascontiguousarray(g("L0_ffn_w1").transpose(0, 2, 1)),
        w3Tall=np.ascontiguousarray(g("L0_ffn_w3").transpose(0, 2, 1)),
        w2Tall=np.ascontiguousarray(g("L0_ffn_w2").transpose(0, 2, 1)),
        ws1T=np.ascontiguousarray(g("L0_ffn_ws1").T),
        ws3T=np.ascontiguousarray(g("L0_ffn_ws3").T),
        ws2T=np.ascontiguousarray(g("L0_ffn_ws2").T),
        c_cols=ccA.host_array(),
        **_moe_consts())
    in_maps = [dict(mapA_common, x=shards[c]) for c in range(NCORES)]
    resA = run_bass_kernel_spmd(ncA, in_maps, core_ids=list(range(NCORES)))
    x1s = [resA.results[c]["x1"] for c in range(NCORES)]
    xn1s = [resA.results[c]["xn1"] for c in range(NCORES)]

    mvecs = np.zeros((128, 96), np.float32)
    cw = g("L1_attn_conv_w")
    mvecs[:, 0:16] = _pack_cols(np.ascontiguousarray(cw[:, 0, 0]))
    mvecs[:, 16:32] = _pack_cols(np.ascontiguousarray(cw[:, 0, 1]))
    mvecs[:, 32:48] = _pack_cols(np.ascontiguousarray(cw[:, 0, 2]))
    mvecs[:, 48:64] = _pack_cols(g("L1_attn_conv_b"))
    mvecs[:, 64:80] = _pack_cols(g("L1_attn_dt_proj_b"))
    mvecs[:, 80:96] = _pack_cols(g("L1_attn_D_param"))
    alog = np.ascontiguousarray(g("L1_attn_A_log").reshape(16, 128, 16).transpose(1, 0, 2).reshape(128, 256))
    sel32 = np.zeros((32, 4096), np.float32)
    for r in range(32):
        sel32[r, r * 128:(r + 1) * 128] = 1.0
    mapB_common = dict(
        in_proj=g("L1_attn_in_proj_w"), x_proj=g("L1_attn_x_proj_w"),
        dt_proj=g("L1_attn_dt_proj_w"), out_proj=g("L1_attn_out_proj_w"),
        mvecs=mvecs, alog=alog, c_sel32=sel32, c_cols=ccB.host_array())
    in_maps = [dict(mapB_common, xn1=xn1s[c]) for c in range(NCORES)]
    resB = run_bass_kernel_spmd(ncB, in_maps, core_ids=list(range(NCORES)))
    s1s = [resB.results[c]["s1"] for c in range(NCORES)]

    perm = _l1_perm()
    W1P = np.ascontiguousarray(g("L1_ffn_W1")[:, :, perm])
    lvecs = np.zeros((128, 108), np.float32)
    b1P = np.zeros((128, 264), np.float32)
    b2v = np.zeros((128, 48), np.float32)
    qs_all, ks_all = g("L1_ffn_q_scale"), g("L1_ffn_k_scale")
    lng, lnb = g("L1_ffn_ln_g"), g("L1_ffn_ln_b")
    b1_all, b2_all = g("L1_ffn_b1"), g("L1_ffn_b2")
    for i in range(6):
        lvecs[:, i * 18:i * 18 + 8] = _pack_cols(lng[i])
        lvecs[:, i * 18 + 8:i * 18 + 16] = _pack_cols(lnb[i])
        lvecs[:, i * 18 + 16] = np.tile(qs_all[i], 2)
        lvecs[:, i * 18 + 17] = np.tile(ks_all[i], 2)
        b1perm = b1_all[i][perm]
        bp = np.zeros((44, 128), np.float32)
        for pc in range(21):
            bp[2 * pc] = b1perm[2 * pc * 128:(2 * pc + 1) * 128]
            bp[2 * pc + 1] = b1perm[(2 * pc + 1) * 128:(2 * pc + 2) * 128]
        bp[42, :42] = b1perm[5376:5418]
        bp[43, :42] = b1perm[5418:5460]
        b1P[:, i * 44:(i + 1) * 44] = bp.T
        b2v[:, i * 8:(i + 1) * 8] = _pack_cols(b2_all[i])
    mapC_common = dict(
        Wqkv=g("L1_ffn_Wqkv"), Wo=g("L1_ffn_Wo"), W1P=W1P, W2=g("L1_ffn_W2"),
        lvecs=lvecs, b1P=b1P, b2v=b2v, c_cols=ccC.host_array(),
        **_l1_consts())
    in_maps = [dict(mapC_common, xn1=xn1s[c], x1=x1s[c], s1=s1s[c]) for c in range(NCORES)]
    resC = run_bass_kernel_spmd(ncC, in_maps, core_ids=list(range(NCORES)))
    out_fm = np.concatenate([resC.results[c]["out"] for c in range(NCORES)], axis=0)
    return np.ascontiguousarray(out_fm.transpose(0, 2, 1))


def _build_fused():
    nc = bass.Bass("TRN2", target_bir_lowering=False, debug=False)
    io = {}
    F32d, BF16d = F32, BF16
    D_ = [NSEQ, D, T]
    io["x"] = nc.dram_tensor("x", D_, F32d, kind="ExternalInput").ap()
    io["wqkvT"] = nc.dram_tensor("wqkvT", [3, 1024, 1024], F32d, kind="ExternalInput").ap()
    io["w1T"] = nc.dram_tensor("w1T", [1024, 4096], F32d, kind="ExternalInput").ap()
    io["w2T"] = nc.dram_tensor("w2T", [4096, 1024], F32d, kind="ExternalInput").ap()
    io["vecs"] = nc.dram_tensor("vecs", [128, 72], F32d, kind="ExternalInput").ap()
    io["ones"] = nc.dram_tensor("c_ones", [128, 128], F32d, kind="ExternalInput").ap()
    io["onesbf"] = nc.dram_tensor("c_onesbf", [128, 128], BF16d, kind="ExternalInput").ap()
    io["gate_wT"] = nc.dram_tensor("gate_wT", [1024, 16], F32d, kind="ExternalInput").ap()
    io["gate_bias"] = nc.dram_tensor("gate_bias", [1, 16], F32d, kind="ExternalInput").ap()
    io["w1Tall"] = nc.dram_tensor("w1Tall", [16, 1024, 768], F32d, kind="ExternalInput").ap()
    io["w3Tall"] = nc.dram_tensor("w3Tall", [16, 1024, 768], F32d, kind="ExternalInput").ap()
    io["w2Tall"] = nc.dram_tensor("w2Tall", [16, 768, 1024], F32d, kind="ExternalInput").ap()
    io["ws1T"] = nc.dram_tensor("ws1T", [1024, 768], F32d, kind="ExternalInput").ap()
    io["ws3T"] = nc.dram_tensor("ws3T", [1024, 768], F32d, kind="ExternalInput").ap()
    io["ws2T"] = nc.dram_tensor("ws2T", [768, 1024], F32d, kind="ExternalInput").ap()
    io["ident"] = nc.dram_tensor("c_ident", [128, 128], F32d, kind="ExternalInput").ap()
    io["sel16"] = nc.dram_tensor("c_sel16", [16, 2048], F32d, kind="ExternalInput").ap()
    for nm in ("iota16", "iota16mb", "iotaj", "iotajmb"):
        io[nm] = nc.dram_tensor("c_" + nm, [128, 16], F32d, kind="ExternalInput").ap()
    for nm in ("iota4", "iota4mb"):
        io[nm] = nc.dram_tensor("c_" + nm, [128, 4], F32d, kind="ExternalInput").ap()
    io["in_proj"] = nc.dram_tensor("in_proj", [1024, 4096], F32d, kind="ExternalInput").ap()
    io["x_proj"] = nc.dram_tensor("x_proj", [2048, 2080], F32d, kind="ExternalInput").ap()
    io["dt_proj"] = nc.dram_tensor("dt_proj", [2048, 2048], F32d, kind="ExternalInput").ap()
    io["out_proj"] = nc.dram_tensor("out_proj", [2048, 1024], F32d, kind="ExternalInput").ap()
    io["mvecs"] = nc.dram_tensor("mvecs", [128, 96], F32d, kind="ExternalInput").ap()
    io["alog"] = nc.dram_tensor("alog", [128, 256], F32d, kind="ExternalInput").ap()
    io["sel32"] = nc.dram_tensor("c_sel32", [32, 4096], F32d, kind="ExternalInput").ap()
    io["Wqkv"] = nc.dram_tensor("Wqkv", [6, 1024, 3072], F32d, kind="ExternalInput").ap()
    io["Wo"] = nc.dram_tensor("Wo", [6, 1024, 1024], F32d, kind="ExternalInput").ap()
    io["W1P"] = nc.dram_tensor("W1P", [6, 1024, 5460], F32d, kind="ExternalInput").ap()
    io["W2"] = nc.dram_tensor("W2", [6, 2730, 1024], F32d, kind="ExternalInput").ap()
    io["lvecs"] = nc.dram_tensor("lvecs", [128, 108], F32d, kind="ExternalInput").ap()
    io["b1P"] = nc.dram_tensor("b1P", [128, 264], F32d, kind="ExternalInput").ap()
    io["b2"] = nc.dram_tensor("b2v", [128, 48], F32d, kind="ExternalInput").ap()
    for nm in ("CS8", "SS8", "CI", "SI"):
        io[nm] = nc.dram_tensor("c_" + nm, [128, T], F32d, kind="ExternalInput").ap()
    io["rqt"] = nc.dram_tensor("c_rqt", [128, 128], BF16d, kind="ExternalInput").ap()
    io["identbf"] = nc.dram_tensor("c_identbf", [128, 128], BF16d, kind="ExternalInput").ap()
    io["ones128"] = nc.dram_tensor("c_ones128", [128, 128], F32d, kind="ExternalInput").ap()
    for nm in ("blk64", "m1", "m2"):
        io[nm] = nc.dram_tensor("c_" + nm, [128, 128], F32d, kind="ExternalInput").ap()
    ccd = nc.dram_tensor("c_cols", [128, 32], F32d, kind="ExternalInput").ap()
    io["out"] = nc.dram_tensor("out", D_, F32d, kind="ExternalOutput").ap()
    for nm, shp, dt in [("xn", D_, F32d), ("s0", D_, F32d), ("qT", D_, BF16d),
                        ("smid", D_, F32d), ("hbufA", [NSEQ, 4096, T], BF16d),
                        ("xnbf", D_, BF16d), ("ybuf", D_, F32d),
                        ("x1", D_, F32d), ("xn1", D_, F32d),
                        ("uraw", [NSEQ, 2048, T], F32d), ("usil", [NSEQ, 2048, T], F32d),
                        ("silg", [NSEQ, 2048, T], BF16d), ("delt", [NSEQ, 2048, T], F32d),
                        ("dpre", [NSEQ, 2048, T], BF16d), ("bc", [NSEQ, 32, T], F32d),
                        ("ymb", [NSEQ, 2048, T], BF16d), ("s1", D_, F32d),
                        ("zbuf", D_, F32d), ("zbuf2", D_, F32d),
                        ("hbufC", [NSEQ, 2816, T], BF16d)]:
        io[nm] = nc.dram_tensor(nm, shp, dt, kind="Internal").ap()
    io["hbuf"] = io["hbufA"]
    ctx = ExitStack()
    with ctx:
        p = Plan(nc, ctx)
        cc = ConstCols(nc, ctx)
        cc.load(p, ccd)
        ones_sb = sb(nc, ctx, "lnA_ones", [128, 128])
        p.add("sync", lambda h, ones_sb=ones_sb, io=io: h.dma_start(out=ones_sb[:], in_=io["ones"]),
              reads=[io["ones"]], writes=[ones_sb], dma=True, tag="ldones")
        emit_outer_ln(p, ctx, io["x"], io["xn"], ones_sb, cc, "ln0")
        emit_l0_attn(p, ctx, io, cc)
        emit_l0_moe(p, ctx, io, cc)
        emit_outer_ln(p, ctx, io["x1"], io["xn1"], ones_sb, cc, "ln1")
        emit_l1_mamba(p, ctx, io, cc)
        io2 = dict(io)
        io2["hbuf"] = io["hbufC"]
        io2["ones"] = io["ones128"]
        sh = L1Shared(p, ctx, io2)
        emit_l1_init_z(p, ctx, io2)
        for li in range(6):
            emit_l1_layer_attn(p, ctx, sh, io2, cc, li)
            emit_l1_layer_ffn(p, ctx, sh, io2, cc, li)
        emit_l1_final(p, ctx, io2, cc)
        p.add("gpsimd", lambda h, io=io: h.nop(), reads=[io["out"]], tag="final")
        p.emit()
    return nc, cc


def kernel_fused(**inputs):
    g = lambda n: np.asarray(inputs[n])
    x = g("L0_attn_x")
    x_fm = np.ascontiguousarray(x.transpose(0, 2, 1))
    shards = [np.ascontiguousarray(x_fm[c * NSEQ:(c + 1) * NSEQ]) for c in range(NCORES)]
    if "F" not in _CACHE:
        _CACHE["F"] = _build_fused()
    ncF, ccF = _CACHE["F"]

    vecs = np.zeros((128, 72), np.float32)
    vecs[:, 0:8] = _pack_cols(g("L0_attn_g1"))
    vecs[:, 8:16] = _pack_cols(g("L0_attn_be1"))
    vecs[:, 16:24] = _pack_cols(g("L0_attn_g2"))
    vecs[:, 24:32] = _pack_cols(g("L0_attn_be2"))
    vecs[:, 32:64] = _pack_cols(g("L0_attn_b1"))
    vecs[:, 64:72] = _pack_cols(g("L0_attn_b2"))
    mvecs = np.zeros((128, 96), np.float32)
    cw = g("L1_attn_conv_w")
    mvecs[:, 0:16] = _pack_cols(np.ascontiguousarray(cw[:, 0, 0]))
    mvecs[:, 16:32] = _pack_cols(np.ascontiguousarray(cw[:, 0, 1]))
    mvecs[:, 32:48] = _pack_cols(np.ascontiguousarray(cw[:, 0, 2]))
    mvecs[:, 48:64] = _pack_cols(g("L1_attn_conv_b"))
    mvecs[:, 64:80] = _pack_cols(g("L1_attn_dt_proj_b"))
    mvecs[:, 80:96] = _pack_cols(g("L1_attn_D_param"))
    alog = np.ascontiguousarray(g("L1_attn_A_log").reshape(16, 128, 16).transpose(1, 0, 2).reshape(128, 256))
    sel32 = np.zeros((32, 4096), np.float32)
    for r in range(32):
        sel32[r, r * 128:(r + 1) * 128] = 1.0
    perm = _l1_perm()
    W1P = np.ascontiguousarray(g("L1_ffn_W1")[:, :, perm])
    lvecs = np.zeros((128, 108), np.float32)
    b1P = np.zeros((128, 264), np.float32)
    b2v = np.zeros((128, 48), np.float32)
    qs_all, ks_all = g("L1_ffn_q_scale"), g("L1_ffn_k_scale")
    lng, lnb = g("L1_ffn_ln_g"), g("L1_ffn_ln_b")
    b1_all, b2_all = g("L1_ffn_b1"), g("L1_ffn_b2")
    for i in range(6):
        lvecs[:, i * 18:i * 18 + 8] = _pack_cols(lng[i])
        lvecs[:, i * 18 + 8:i * 18 + 16] = _pack_cols(lnb[i])
        lvecs[:, i * 18 + 16] = np.tile(qs_all[i], 2)
        lvecs[:, i * 18 + 17] = np.tile(ks_all[i], 2)
        b1perm = b1_all[i][perm]
        bp = np.zeros((44, 128), np.float32)
        for pc in range(21):
            bp[2 * pc] = b1perm[2 * pc * 128:(2 * pc + 1) * 128]
            bp[2 * pc + 1] = b1perm[(2 * pc + 1) * 128:(2 * pc + 2) * 128]
        bp[42, :42] = b1perm[5376:5418]
        bp[43, :42] = b1perm[5418:5460]
        b1P[:, i * 44:(i + 1) * 44] = bp.T
        b2v[:, i * 8:(i + 1) * 8] = _pack_cols(b2_all[i])
    common = dict(
        wqkvT=np.ascontiguousarray(np.stack([g("L0_attn_wq").T, g("L0_attn_wk").T, g("L0_attn_wv").T])),
        w1T=np.ascontiguousarray(g("L0_attn_w1").T),
        w2T=np.ascontiguousarray(g("L0_attn_w2").T),
        vecs=vecs,
        c_ones=np.ones((128, 128), np.float32),
        c_onesbf=np.ones((128, 128), ml_dtypes.bfloat16),
        gate_wT=np.ascontiguousarray(g("L0_ffn_gate_w").T),
        gate_bias=np.ascontiguousarray(g("L0_ffn_gate_bias").reshape(1, 16)),
        w1Tall=np.ascontiguousarray(g("L0_ffn_w1").transpose(0, 2, 1)),
        w3Tall=np.ascontiguousarray(g("L0_ffn_w3").transpose(0, 2, 1)),
        w2Tall=np.ascontiguousarray(g("L0_ffn_w2").transpose(0, 2, 1)),
        ws1T=np.ascontiguousarray(g("L0_ffn_ws1").T),
        ws3T=np.ascontiguousarray(g("L0_ffn_ws3").T),
        ws2T=np.ascontiguousarray(g("L0_ffn_ws2").T),
        in_proj=g("L1_attn_in_proj_w"), x_proj=g("L1_attn_x_proj_w"),
        dt_proj=g("L1_attn_dt_proj_w"), out_proj=g("L1_attn_out_proj_w"),
        mvecs=mvecs, alog=alog, c_sel32=sel32,
        Wqkv=g("L1_ffn_Wqkv"), Wo=g("L1_ffn_Wo"), W1P=W1P, W2=g("L1_ffn_W2"),
        lvecs=lvecs, b1P=b1P, b2v=b2v,
        c_cols=ccF.host_array(),
        **_moe_consts(), **_l1_consts())
    in_maps = [dict(common, x=shards[c]) for c in range(NCORES)]
    res = run_bass_kernel_spmd(ncF, in_maps, core_ids=list(range(NCORES)))
    out_fm = np.concatenate([res.results[c]["out"] for c in range(NCORES)], axis=0)
    return np.ascontiguousarray(out_fm.transpose(0, 2, 1))
